# version 26
# baseline (speedup 1.0000x reference)
import numpy as np
import ml_dtypes
from contextlib import ExitStack

import concourse.bass as bass
import concourse.mybir as mybir
from concourse.bass_utils import run_bass_kernel_spmd

F32 = mybir.dt.float32
BF16 = mybir.dt.bfloat16
I32 = mybir.dt.int32
AF = mybir.ActivationFunctionType
ALU = mybir.AluOpType
AX = mybir.AxisListType
ts = bass.ts

S = 4096
D = 1024
NH = 16
HD = 64
QS = 512
NQS = S // QS
DC = D // 128
NKT = S // 128

MAGIC = 0x5F3759DF

ENGINES = ("tensor", "vector", "scalar", "gpsimd", "sync")


class Sem:
    __slots__ = ("h", "n")

    def __init__(self, handle):
        self.h = handle
        self.n = 0


class Step:
    __slots__ = ("kind", "op", "args", "kwargs", "sem", "val")

    def __init__(self, kind, op=None, args=None, kwargs=None, sem=None, val=0):
        self.kind = kind
        self.op = op
        self.args = args
        self.kwargs = kwargs
        self.sem = sem
        self.val = val


class Eng:

    def __init__(self, prog, name):
        self.P = prog
        self.name = name

    def wait(self, sem, val=None):
        if val is None:
            val = sem.n
        if val > 0:
            self.P.steps[self.name].append(Step("wait", sem=sem, val=val))

    def op(self, opname, *args, inc=None, inc_by=1, **kwargs):
        st = Step("op", op=opname, args=args, kwargs=kwargs)
        if inc is not None:
            st.sem = inc
            st.val = inc_by
            inc.n += inc_by
        self.P.steps[self.name].append(st)
        return st

    def dma(self, out, in_, inc, **kwargs):
        return self.op("dma_start", out, in_, inc=inc, inc_by=16, **kwargs)

    def __getattr__(self, opname):
        def rec(*args, inc=None, inc_by=1, **kwargs):
            return self.op(opname, *args, inc=inc, inc_by=inc_by, **kwargs)
        return rec


class FencedEng:

    def __init__(self, eng, sem):
        self.e = eng
        self.s = sem

    def wait(self, sem, val=None):
        self.e.wait(sem, val)

    def __getattr__(self, opname):
        def rec(*args, inc=None, inc_by=1, **kwargs):
            st = self.e.op(opname, *args, inc=self.s, **kwargs)
            self.e.wait(self.s, self.s.n)
            if inc is not None:
                inc.n += inc_by
                st2 = Step("op", op="nop", args=(), kwargs={})
                st2.sem = inc
                st2.val = inc_by
                self.e.P.steps[self.e.name].append(st2)
            return st
        return rec


class Bank:
    __slots__ = ("ap", "free_sem", "free_val")

    def __init__(self, ap):
        self.ap = ap
        self.free_sem = None
        self.free_val = 0

    def acquire(self, eng):
        if self.free_sem is not None:
            eng.wait(self.free_sem, self.free_val)

    def release(self, sem, val=None):
        self.free_sem = sem
        self.free_val = sem.n if val is None else val


class Prog:
    def __init__(self, name="mk"):
        self.nc = bass.Bass("TRN2", target_bir_lowering=False, name=name)
        self.steps = {e: [] for e in ENGINES}
        self.ctx = ExitStack()
        self._sems = {}
        self.te = Eng(self, "tensor")
        self.ve = Eng(self, "vector")
        self.se = Eng(self, "scalar")
        self.ge = Eng(self, "gpsimd")
        self.sy = Eng(self, "sync")
        self.vf = FencedEng(self.ve, self.sem("s_vf"))
        self.sf = FencedEng(self.se, self.sem("s_af"))

    def sem(self, name):
        if name not in self._sems:
            h = self.ctx.enter_context(self.nc.semaphore(name))
            self._sems[name] = Sem(h)
        return self._sems[name]

    def sb(self, name, shape, dtype):
        return self.ctx.enter_context(self.nc.sbuf_tensor(name, shape, dtype))

    def ps(self, name, shape, dtype=F32):
        return self.ctx.enter_context(self.nc.psum_tensor(name, shape, dtype))

    def dram(self, name, shape, dtype, kind="Internal"):
        return self.nc.dram_tensor(name, shape, dtype, kind=kind)

    def barrier(self):
        b = self.sem("bar")
        for e in ENGINES:
            st = Step("op", op="nop", args=(), kwargs={})
            st.sem = b
            st.val = 1
            self.steps[e].append(st)
        b.n += 5
        for e in ENGINES:
            self.steps[e].append(Step("wait", sem=b, val=b.n))

    def realize(self):
        nc = self.nc
        with nc.Block() as block:
            for ename in ENGINES:
                steps = self.steps[ename]

                def make(fn_steps):
                    def run(eng):
                        for st in fn_steps:
                            if st.kind == "wait":
                                eng.wait_ge(st.sem.h, st.val)
                            else:
                                try:
                                    if st.op == "reciprocal":
                                        with nc.allow_low_precision(
                                                reason="bf16 softmax normalizer"):
                                            ins = getattr(eng, st.op)(
                                                *st.args, **st.kwargs)
                                    else:
                                        ins = getattr(eng, st.op)(*st.args,
                                                                  **st.kwargs)
                                except Exception:
                                    import sys
                                    print(f"FAILED {ename} op={st.op} "
                                          f"args={[type(a).__name__ for a in st.args]} "
                                          f"kwargs={list(st.kwargs)}",
                                          file=sys.stderr)
                                    raise
                                if st.sem is not None:
                                    ins.then_inc(st.sem.h, st.val)
                    return run
                getattr(block, ename)(make(steps))
        self.ctx.close()
        return nc



def emit_ln(P, out_ap, psum_ap, res_ap, y_sb, yjunk, st8, magic=None, inc=None,
            premul=None):
    v, se = P.vf, P.sf
    s_ae = P.sem("s_ae")
    s_ve = P.sem("s_ve")
    n = 1024
    if not isinstance(y_sb, bass.AP):
        y_sb = y_sb[:, :]
    if not isinstance(yjunk, bass.AP):
        yjunk = yjunk[:, :]
    if premul is not None:
        v.tensor_tensor(yjunk, psum_ap, premul, ALU.mult)
        v.scalar_tensor_tensor(y_sb, yjunk, 1.0, res_ap, ALU.mult, ALU.add,
                               accum_out=st8[:, 0:1])
    else:
        v.scalar_tensor_tensor(y_sb, psum_ap, 1.0, res_ap, ALU.mult, ALU.add,
                               accum_out=st8[:, 0:1])
    v.scalar_tensor_tensor(yjunk, y_sb, 1.0, y_sb, ALU.mult, ALU.mult,
                           accum_out=st8[:, 1:2])
    v.tensor_single_scalar(st8[:, 4:5], st8[:, 0:1], 1.0 / n, ALU.mult)
    v.tensor_tensor(st8[:, 2:3], st8[:, 4:5], st8[:, 4:5], ALU.mult)
    v.tensor_scalar(st8[:, 1:2], st8[:, 1:2], 1.0 / n, 1e-5, ALU.mult, ALU.add)
    v.tensor_tensor(st8[:, 2:3], st8[:, 1:2], st8[:, 2:3], ALU.subtract,
                    inc=s_ve)
    se.wait(P.sem("s_vf"))
    se.activation(st8[:, 5:6], st8[:, 2:3], AF.Ln)
    se.activation(st8[:, 3:4], st8[:, 5:6], AF.Exp, scale=-0.5, inc=s_ae)
    v.wait(s_ae)
    v.tensor_tensor(st8[:, 4:5], st8[:, 4:5], st8[:, 3:4], ALU.mult)
    v.tensor_single_scalar(st8[:, 4:5], st8[:, 4:5], -1.0, ALU.mult)
    v.tensor_scalar(out_ap, y_sb, st8[:, 3:4], st8[:, 4:5], ALU.mult, ALU.add,
                    inc=inc)


def make_consts():
    bf = ml_dtypes.bfloat16
    E = np.zeros((128, DC * 16), dtype=bf)
    for c in range(DC):
        for p in range(128):
            h = (128 * c + p) // HD
            E[p, c * 16 + h] = 1
    I16 = np.zeros((16, 16), dtype=bf)
    np.fill_diagonal(I16, 1)
    ident = np.eye(128, dtype=bf)
    ones1 = np.ones((128, 128), dtype=bf)
    ki = np.arange(128)[:, None]
    j0 = np.arange(1408)[None, :]
    mL0 = (np.abs(ki - (j0 - 640)) <= 256).astype(bf)
    j1 = np.arange(896)[None, :]
    mL1 = ((j1 - 384) >= ki).astype(bf)
    return dict(cE=E, cI16=I16, cones=ones1, cmL0=mL0, cmL1=mL1, cident=ident)


def emit_consts(P):
    nc = P.nc
    d = {}
    d["E_d"] = P.dram("cE", [128, DC * 16], BF16, kind="ExternalInput")
    d["I16_d"] = P.dram("cI16", [16, 16], BF16, kind="ExternalInput")
    d["ones_d"] = P.dram("cones", [128, 128], BF16, kind="ExternalInput")
    d["mL0_d"] = P.dram("cmL0", [128, 1408], BF16, kind="ExternalInput")
    d["mL1_d"] = P.dram("cmL1", [128, 896], BF16, kind="ExternalInput")
    d["ident_d"] = P.dram("cident", [128, 128], BF16, kind="ExternalInput")
    cs = {}
    cs["E"] = P.sb("c_E", [128, DC * 16], BF16)
    cs["I16"] = P.sb("c_I16", [16, 16], BF16)
    cs["ones1"] = P.sb("c_ones", [128, 128], BF16)
    cs["ident"] = P.sb("c_ident", [128, 128], BF16)
    cs["mL0_d"] = d["mL0_d"]
    cs["mL1_d"] = d["mL1_d"]

    d_c = P.sem("d_c")
    P.sy.dma(cs["E"][:, :], d["E_d"][:, :], inc=d_c)
    P.sy.dma(cs["I16"][:, :], d["I16_d"][:, :], inc=d_c)
    P.sy.dma(cs["ones1"][:, :], d["ones_d"][:, :], inc=d_c)
    P.sy.dma(cs["ident"][:, :], d["ident_d"][:, :], inc=d_c)
    ge_done = P.sem("s_misc")
    P.ge.nop(inc=ge_done)
    for e in (P.te, P.ve, P.se, P.ge):
        e.wait(d_c)
        e.wait(ge_done)
    return cs



def emit_attn(P, cs, xsrc, xdst, srcs, mode, pfx, debug=None):
    te, ve, se, ge, sy = P.te, P.ve, P.se, P.ge, P.sy
    nc = P.nc
    l0 = (mode == "l0")
    nslot = 3 if l0 else NQS

    phase_ctx = ExitStack()

    def sb(name, shape, dtype):
        return phase_ctx.enter_context(nc.sbuf_tensor(pfx + name, shape, dtype))

    def ps(name, shape, dtype=F32):
        return phase_ctx.enter_context(nc.psum_tensor(pfx + name, shape, dtype))

    qp = getattr(P, "_p1_qprep", None) if l0 else None
    if qp is not None:
        qp.alloc(phase_ctx)

    kT = sb("kT", [128, DC * nslot * QS], BF16)
    vv = sb("vv", [128, nslot * 4 * 1040], BF16)
    x_fm = [sb(f"xfm{i}", [128, DC * QS], BF16) for i in range(2)]
    sq = sb("sq", [128, QS], BF16)
    mk2 = sb("mk2", [16, 16], F32)

    pvp = [ps(f"pvp{i}", [128, 1024], F32).ap() for i in range(2)]
    pv_ps = [Bank(pvp[0][:, 0:512]), Bank(pvp[0][:, 512:1024]),
             Bank(pvp[1][:, 0:512]), Bank(pvp[1][:, 512:1024])]
    s01 = ps("s01", [128, 1024], F32).ap()
    s_ps = [Bank(s01[:, 0:512]), Bank(s01[:, 512:1024])]
    o_ps = Bank(ps("o", [128, 1024], F32).ap())
    ovA = Bank(o_ps.ap[:, 0:512])
    ovB = Bank(o_ps.ap[:, 512:1024])

    d_w = P.sem(pfx + "d_w")
    d_xf = [P.sem(pfx + f"d_xf{i}") for i in range(2)]
    d_xt = [P.sem(pfx + f"d_xt{i}") for i in range(2)]
    d_out = P.sem("d_out")
    s_pe = P.sem("s_pe")
    s_ae = P.sem("s_ae")
    s_ve = P.sem("s_ve")
    s_ge = P.sem("s_ge")
    s_gp = P.sem("s_gp")

    xf_loads = [0, 0]

    def load_xfm(s):
        b = s % 2
        for c in range(DC):
            sy.dma(x_fm[b][:, c * QS:(c + 1) * QS],
                   xsrc[s * QS:(s + 1) * QS, c * 128:(c + 1) * 128],
                   inc=d_xf[b], transpose=True)
        xf_loads[b] = d_xf[b].n

    for t in range(nslot):
        for j in range(4):
            base = (t * 4 + j) * 1040
            ge.memset(vv[:, base:base + 1040].rearrange(
                "p (h w) -> p h w", h=NH)[:, :, 64:65], 1.0)
    ge.memset(mk2[0:16, 0:1], 0.0, inc=s_ge)
    te.wait(s_ge)
    ve.wait(s_ge)

    def kv_slot(sl, wk_ap, wv_ap, xb, xb_sem, xb_val):
        t = sl % nslot
        te.wait(xb_sem, xb_val)
        for oc in range(DC):
            bk = s_ps[oc % 2]
            bk.acquire(te)
            for ic in range(DC):
                te.matmul(bk.ap, wk_ap(ic, oc),
                          xb[:, ic * QS:(ic + 1) * QS],
                          start=(ic == 0), stop=(ic == DC - 1),
                          inc=s_pe if ic == DC - 1 else None)
            se.wait(s_pe)
            se.activation(kT[:, (oc * nslot + t) * QS:(oc * nslot + t + 1) * QS],
                          bk.ap, AF.Copy, inc=s_ae)
            bk.release(s_ae)
        bk = s_ps[0]
        bk.acquire(te)
        for oc in range(DC):
            se.wait(s_ae)
            se.activation(sq[:, :],
                          kT[:, (oc * nslot + t) * QS:(oc * nslot + t + 1) * QS],
                          AF.Square, inc=s_ae)
            te.wait(s_ae)
            te.matmul(bk.ap[0:16, :], cs["E"][:, oc * 16:(oc + 1) * 16],
                      sq[:, :], start=(oc == 0), stop=(oc == DC - 1),
                      inc=s_pe if oc == DC - 1 else None)
        ve.wait(s_pe)
        P.vf.tensor_reduce(mk2[0:16, 1:2], bk.ap[0:16, :], AX.X, ALU.max)
        P.vf.tensor_tensor(mk2[0:16, 0:1], mk2[0:16, 0:1], mk2[0:16, 1:2],
                           ALU.max, inc=s_ve)
        bk.release(s_ve)
        for j in range(4):
            for half in range(2):
                bk = s_ps[half]
                bk.acquire(te)
                for ic in range(DC):
                    te.matmul(bk.ap,
                              xb[:, ic * QS + j * 128:ic * QS + (j + 1) * 128],
                              wv_ap(ic, half),
                              start=(ic == 0), stop=(ic == DC - 1),
                              inc=s_pe if ic == DC - 1 else None)
            base = (t * 4 + j) * 1040
            ve.wait(s_pe)
            for half in range(2):
                dst = vv[:, base:base + 1040].rearrange(
                    "p (h w) -> p h w", h=NH)[:, half * 8:(half + 1) * 8, 0:64]
                ve.tensor_copy(dst, s_ps[half].ap.rearrange(
                    "p (h w) -> p h w", w=64), inc=s_ve)
                s_ps[half].release(s_ve)

    if l0:
        wqkvT = sb("wqkvT", [128, DC * 3072], BF16)
        for c in range(DC):
            sy.dma(wqkvT[:, c * 3072:(c + 1) * 3072],
                   srcs["qkv"][:, c * 128:(c + 1) * 128], inc=d_w, transpose=True)
        woT = sb("woT", [128, DC * 1024], BF16)
        for c in range(DC):
            sy.dma(woT[:, c * 1024:(c + 1) * 1024],
                   srcs["o"][:, c * 128:(c + 1) * 128], inc=d_w, transpose=True)

        def wk_ap(ic, oc):
            return wqkvT[:, ic * 3072 + 1024 + oc * 128:ic * 3072 + 1024 + (oc + 1) * 128]

        def wv_ap(ic, half):
            return wqkvT[:, ic * 3072 + 2048 + half * 512:ic * 3072 + 2048 + (half + 1) * 512]

        def wq_ap(ic, oc):
            return wqkvT[:, ic * 3072 + oc * 128:ic * 3072 + (oc + 1) * 128]

        def wo_ap(ic, half):
            return woT[:, ic * 1024 + half * 512:ic * 1024 + (half + 1) * 512]

        te.wait(d_w)
        se.wait(d_w)
        load_xfm(0)
        load_xfm(1)
        kv_slot(0, wk_ap, wv_ap, x_fm[0], d_xf[0], xf_loads[0])
        if NQS > 1:
            kv_slot(1, wk_ap, wv_ap, x_fm[1], d_xf[1], xf_loads[1])
    else:
        pre_ctx = ExitStack()
        wkT = pre_ctx.enter_context(nc.sbuf_tensor(pfx + "wkT", [128, DC * 1024], BF16))
        wvT = pre_ctx.enter_context(nc.sbuf_tensor(pfx + "wvT", [128, DC * 1024], BF16))
        for c in range(DC):
            sy.dma(wkT[:, c * 1024:(c + 1) * 1024],
                   srcs["k"][:, c * 128:(c + 1) * 128], inc=d_w, transpose=True)
            sy.dma(wvT[:, c * 1024:(c + 1) * 1024],
                   srcs["v"][:, c * 128:(c + 1) * 128], inc=d_w, transpose=True)

        def wk_ap(ic, oc):
            return wkT[:, ic * 1024 + oc * 128:ic * 1024 + (oc + 1) * 128]

        def wv_ap(ic, half):
            return wvT[:, ic * 1024 + half * 512:ic * 1024 + (half + 1) * 512]

        te.wait(d_w)
        se.wait(d_w)
        load_xfm(0)
        for sl in range(NQS):
            if sl + 1 < NQS:
                if sl > 0:
                    sy.wait(s_pe)
                load_xfm(sl + 1)
            kv_slot(sl, wk_ap, wv_ap, x_fm[sl % 2], d_xf[sl % 2],
                    xf_loads[sl % 2])
        P.barrier()
        pre_ctx.close()
        wbuf = sb("wbuf", [128, DC * 1024], BF16)
        wq_chunk = [0] * DC
        wo_loaded = [0]

        def wq_ap(ic, oc):
            return wbuf[:, ic * 1024 + oc * 128:ic * 1024 + (oc + 1) * 128]

        def wo_ap(ic, half):
            return wbuf[:, ic * 1024 + half * 512:ic * 1024 + (half + 1) * 512]

    q_fm = sb("qfm", [128, DC * QS], BF16)
    p_buf = sb("pbuf", [128, 12 * QS], BF16)
    ao_fm = sb("ao", [128, DC * QS], BF16)
    x_tm = [sb(f"xtm{i}", [128, 1024], BF16) for i in range(2)]
    recipb = sb("recipb", [128, 3 * QS], BF16)
    small = sb("small", [128, 32], F32)
    mrow = sb("mrow", [16, 16], BF16)
    y_sb = sb("y", [128, 1024], F32)
    yjunk = sb("yjunk", [128, 1024], BF16)
    st8 = sb("st8", [128, 8], F32)
    x1t = [sb("x1t0", [128, 1024], BF16)] * 2
    mwide = 1408 if l0 else 896
    mbase = 896 if l0 else 384
    masksb = sb("masks", [128, mwide], BF16)
    d_msk = P.sem(pfx + "d_msk")
    sy.dma(masksb[:, :], (cs["mL0_d"] if l0 else cs["mL1_d"])[:, :], inc=d_msk)
    ve.wait(d_msk)

    exp_done = {}
    pv_done = {}
    consumed_sem = {}
    consumed_val = {}
    ready_sem = {}
    ready_val = {}
    gu = [0]
    for s in range(NQS):
        if not l0:
            sy.wait(s_pe)
            for c in range(DC):
                sy.dma(wbuf[:, c * 1024:(c + 1) * 1024],
                       srcs["q"][:, c * 128:(c + 1) * 128], inc=d_w,
                       transpose=True)
                wq_chunk[c] = d_w.n
            if s == 0:
                load_xfm(0)
        else:
            if s + 1 < NQS and s > 0:
                kv_slot(s + 1, wk_ap, wv_ap, x_fm[(s + 1) % 2],
                        d_xf[(s + 1) % 2], xf_loads[(s + 1) % 2])
        te.wait(d_xf[s % 2], xf_loads[s % 2])
        ovA.free_sem, ovA.free_val = o_ps.free_sem, o_ps.free_val
        ovB.free_sem, ovB.free_val = o_ps.free_sem, o_ps.free_val
        all8 = [s_ps[0], s_ps[1], ovA, ovB] + pv_ps
        for oc in range(DC):
            all8[oc].acquire(te)
        for ic in range(DC):
            if not l0:
                te.wait(d_w, wq_chunk[ic])
            for oc in range(DC):
                te.matmul(all8[oc].ap, wq_ap(ic, oc),
                          x_fm[s % 2][:, ic * QS:(ic + 1) * QS],
                          start=(ic == 0), stop=(ic == DC - 1),
                          inc=s_pe if ic == DC - 1 and oc == DC - 1 else None)
        se.wait(s_pe)
        for oc in range(DC):
            se.activation(q_fm[:, oc * QS:(oc + 1) * QS], all8[oc].ap,
                          AF.Copy, inc=s_ae)
            all8[oc].release(s_ae)
        if not l0:
            sy.wait(s_pe)
            for c in range(DC):
                sy.dma(wbuf[:, c * 1024:(c + 1) * 1024],
                       srcs["o"][:, c * 128:(c + 1) * 128], inc=d_w,
                       transpose=True)
            wo_loaded[0] = d_w.n
        nxt = s + 2 if l0 else s + 1
        if nxt < NQS:
            sy.wait(s_pe)
            load_xfm(nxt)

        bk = s_ps[0]
        bk.acquire(te)
        for oc in range(DC):
            se.wait(s_ae)
            se.activation(sq[:, :], q_fm[:, oc * QS:(oc + 1) * QS], AF.Square,
                          inc=s_ae)
            te.wait(s_ae)
            te.matmul(bk.ap[0:16, :], cs["E"][:, oc * 16:(oc + 1) * 16],
                      sq[:, :], start=(oc == 0), stop=(oc == DC - 1),
                      inc=s_pe if oc == DC - 1 else None)
        ve.wait(s_pe)
        vf, sf = P.vf, P.sf
        vf.tensor_reduce(mk2[0:16, 2:3], bk.ap[0:16, :], AX.X, ALU.max)
        vf.tensor_tensor(mk2[0:16, 3:4], mk2[0:16, 2:3], mk2[0:16, 0:1],
                         ALU.mult)
        vf.tensor_scalar(mk2[0:16, 3:4], mk2[0:16, 3:4], 1.0 / (8.0 * 8.0),
                         1e-30, ALU.mult, ALU.max)
        sf.wait(P.sem("s_vf"))
        sf.activation(mk2[0:16, 4:5], mk2[0:16, 3:4], AF.Ln)
        sf.activation(mk2[0:16, 5:6], mk2[0:16, 4:5], AF.Exp, scale=0.5,
                      inc=s_ae)
        vf.wait(s_ae)
        vf.tensor_single_scalar(mk2[0:16, 6:7], mk2[0:16, 5:6], -1.0, ALU.mult)
        vf.tensor_copy(mrow[0:16, 0:1], mk2[0:16, 6:7], inc=s_ve)
        te.wait(s_ve)
        te.matmul(bk.ap[0:1, 0:16], mrow[0:16, 0:1], cs["I16"][0:16, 0:16],
                  start=True, stop=True, inc=s_pe)
        se.wait(s_pe)
        se.activation(mrow[0:1, 0:16], bk.ap[0:1, 0:16], AF.Copy, inc=s_ae)
        te.wait(s_ae)
        te.matmul(bk.ap[:, 0:16], cs["ones1"][0:1, 0:128], mrow[0:1, 0:16],
                  start=True, stop=True, inc=s_pe)
        se.wait(s_pe)
        se.activation(small[:, 0:16], bk.ap[:, 0:16], AF.Copy, inc=s_ae)
        bias_ready = s_ae.n
        bk.release(s_ae)
        ve.wait(s_ae, bias_ready)
        P.vf.tensor_tensor(small[:, 16:24],
                           small[:, 0:16].rearrange("p (a b) -> p a b", b=2)[:, :, 0],
                           small[:, 0:16].rearrange("p (a b) -> p a b", b=2)[:, :, 1],
                           ALU.min, inc=s_ve)
        te.wait(s_ve)
        se.wait(s_ve)

        if l0:
            kts = list(range(max(0, 4 * s - 2), min(NKT, 4 * s + 6)))
        else:
            kts = list(range(0, 4 * (s + 1)))
        pend_epi = [None]
        slot_exp = {}

        def emit_epilogue(w2):
            hA, hB = 2 * w2, 2 * w2 + 1
            bks = [pv_ps[2 * (w2 % 2)], pv_ps[2 * (w2 % 2) + 1]]
            sl = [slots[2 * (w2 % 2)], slots[2 * (w2 % 2) + 1]]
            ve.wait(s_pe, pv_last[w2])
            for z in (0, 1):
                ve.reciprocal(sl[z][64:65, :], bks[z].ap[64:65, :],
                              inc=s_ve if z == 1 else None)
            te.wait(s_ve)
            for z in (0, 1):
                te.matmul(bks[z].ap[64:128, :], cs["ones1"][64:65, 0:64],
                          sl[z][64:65, :], start=True, stop=True,
                          inc=s_pe if z == 1 else None)
            se.wait(s_pe)
            for z in (0, 1):
                se.activation(sl[z][0:64, :], bks[z].ap[64:128, :], AF.Copy,
                              inc=s_ae if z == 1 else None)
            ve.wait(s_ae)
            for z in (0, 1):
                h = (hA, hB)[z]
                c = h // 2
                ro = (h % 2) * 64
                ve.tensor_tensor(ao_fm[ro:ro + 64, c * QS:(c + 1) * QS],
                                 bks[z].ap[0:64, :], sl[z][0:64, :], ALU.mult,
                                 inc=s_ve)
                bks[z].release(s_ve)

        slots = [recipb[:, 0:QS], recipb[:, QS:2 * QS],
                 recipb[:, 2 * QS:3 * QS], sq[:, 0:QS]]
        pv_last = {}
        for w in range(8):
            bkA = pv_ps[2 * (w % 2)]
            bkB = pv_ps[2 * (w % 2) + 1]
            bkA.acquire(te)
            bkB.acquire(te)
            te.wait(s_ae, slot_exp.get(2 + (w % 2), 0))
            sl3 = [(0, s01, (s_ps[0], s_ps[1])),
                   (1, o_ps.ap, (ovA, ovB)),
                   (2 + ((w + 1) % 2), pvp[(w + 1) % 2],
                    (pv_ps[2 * ((w + 1) % 2)], pv_ps[2 * ((w + 1) % 2) + 1]))]
            fresh = {k: True for k in range(3)}
            LAG = 2
            pend = []

            def emit_pv(info):
                i2, kt2, pb2 = info
                t2 = (kt2 // 4) % nslot
                te.wait(ready_sem[i2], ready_val[i2])
                vbase = (t2 * 4 + (kt2 % 4)) * 1040
                for z in (0, 1):
                    h2 = 2 * w + z
                    vb = vv[:, vbase:vbase + 1040].rearrange(
                        "p (h x) -> p h x", h=NH)[:, h2, :]
                    te.matmul((bkA, bkB)[z].ap[0:65, :], vb,
                              pb2[:, z * QS:(z + 1) * QS],
                              start=(kt2 == kts[0]), stop=(kt2 == kts[-1]),
                              inc=s_pe)
                pv_done[i2] = s_pe.n
                if consumed_sem[i2] is s_pe:
                    consumed_val[i2] = s_pe.n

            for iw, kt in enumerate(kts):
                i = gu[0]
                gu[0] += 1
                slot_id, pbk, halves = sl3[iw % 3]
                c = w
                t = (kt // 4) % nslot
                within = (kt % 4) * 128
                if fresh[iw % 3]:
                    fresh[iw % 3] = False
                    halves[0].acquire(te)
                    halves[1].acquire(te)
                    te.wait(s_ae, slot_exp.get(slot_id, 0))
                else:
                    te.wait(s_ae, exp_done[i - 3])
                for z in (0, 1):
                    ro = z * 64
                    lhsT = kT[ro:ro + 64,
                              (c * nslot + t) * QS + within:
                              (c * nslot + t) * QS + within + 128]
                    rhs = q_fm[ro:ro + 64, c * QS:(c + 1) * QS]
                    te.matmul(pbk[:, z * QS:(z + 1) * QS], lhsT, rhs,
                              start=True, stop=True,
                              inc=s_pe if z == 1 else None)
                se.wait(s_pe)
                pslot = p_buf[:, (i % 3) * 1024:(i % 3) * 1024 + 1024]
                if i >= 3:
                    se.wait(consumed_sem[i - 3], consumed_val[i - 3])
                se.activation(pslot, pbk[:, 0:1024], AF.Exp,
                              bias=small[:, 16 + w:17 + w],
                              scale=0.125, inc=s_ae)
                exp_done[i] = s_ae.n
                slot_exp[slot_id] = s_ae.n
                delta = kt * 128 - s * QS
                need_mask = l0 or (delta >= 0)
                if need_mask:
                    mi = (delta + 256) // 128 if l0 else delta // 128
                    mslot = p_buf[:, 3072 + (i % 3) * 1024:
                                  3072 + (i % 3) * 1024 + 1024]
                    ge.wait(s_ae, exp_done[i])
                    if i >= 3:
                        ge.wait(s_pe, pv_done[i - 3])
                    mlo = mbase - 128 * mi
                    ge.tensor_tensor(
                        mslot.rearrange("p (z q) -> p z q", z=2),
                        pslot.rearrange("p (z q) -> p z q", z=2),
                        masksb[:, mlo:mlo + QS].unsqueeze(1)
                        .broadcast_to([128, 2, QS]),
                        ALU.mult, inc=s_gp)
                    ready_sem[i] = s_gp
                    ready_val[i] = s_gp.n
                    consumed_sem[i] = s_gp
                    consumed_val[i] = s_gp.n
                    pslot = mslot
                else:
                    ready_sem[i] = s_ae
                    ready_val[i] = exp_done[i]
                    consumed_sem[i] = s_pe
                pend.append((i, kt, pslot))
                if len(pend) > LAG:
                    emit_pv(pend.pop(0))
                if iw == 1 and pend_epi[0] is not None:
                    pend_epi[0]()
                    pend_epi[0] = None
            while pend:
                emit_pv(pend.pop(0))
            pv_last[w] = s_pe.n
            if pend_epi[0] is not None:
                pend_epi[0]()
            pend_epi[0] = (lambda w2=w: emit_epilogue(w2))
        pend_epi[0]()
        pend_epi[0] = None
        for bk4 in (s_ps[0], s_ps[1], ovA, ovB):
            bk4.release(s_ae)
        o_ps.release(s_ae)
        pv_ps[0].release(s_ae)
        pv_ps[1].release(s_ae)

        te.wait(s_ve)
        if not l0:
            te.wait(d_w, wo_loaded[0])
        for j in range(4):
            bsel = (s * 4 + j) % 2
            sy.wait(s_ve)
            src = xsrc[(s * QS + j * 128):(s * QS + (j + 1) * 128), :]
            sy.dma(x_tm[bsel][:, :], src, inc=d_xt[bsel])
            o_ps.acquire(te)
            for half in range(2):
                for ic in range(DC):
                    te.matmul(o_ps.ap[:, half * 512:(half + 1) * 512],
                              ao_fm[:, ic * QS + j * 128:ic * QS + (j + 1) * 128],
                              wo_ap(ic, half),
                              start=(ic == 0), stop=(ic == DC - 1),
                              inc=s_pe if ic == DC - 1 else None)
            ve.wait(s_pe)
            ve.wait(d_xt[bsel])
            ve.wait(d_out)
            emit_ln(P, x1t[bsel][:, :], o_ps.ap[:, :], x_tm[bsel][:, :],
                    y_sb, yjunk, st8, inc=s_ve)
            o_ps.release(s_ve)
            sy.wait(s_ve)
            sy.dma(xdst[(s * QS + j * 128):(s * QS + (j + 1) * 128), :],
                   x1t[bsel][:, :], inc=d_out)

    if debug is not None:
        d_dbg = P.sem(pfx + "d_dbg")
        sy.wait(s_ve)
        sy.wait(s_ae)
        if "q_fm" in debug:
            sy.dma(debug["q_fm"][:, :], q_fm[:, :], inc=d_dbg)
        if "ao_fm" in debug:
            sy.dma(debug["ao_fm"][:, :], ao_fm[:, :], inc=d_dbg)
        if "small" in debug:
            sy.dma(debug["small"][:, :], small[:, :], inc=d_dbg)
        if "mk2" in debug:
            sy.dma(debug["mk2"][:, :], mk2[:, :], inc=d_dbg)
        if "kT" in debug:
            sy.dma(debug["kT"][:, :], kT[:, :], inc=d_dbg)
        if "vv" in debug:
            sy.dma(debug["vv"][:, :], vv[:, :], inc=d_dbg)
        if "pbuf" in debug:
            sy.dma(debug["pbuf"][:, :], p_buf[:, :], inc=d_dbg)
        sy.wait(d_dbg)
    sy.wait(d_out)
    P.barrier()
    phase_ctx.close()


def _apslice(ap, lo, hi):
    if not isinstance(ap, bass.AP):
        return ap[:, lo:hi]
    t = ap
    return bass.AP(t.tensor, t.offset + lo, [list(t.ap[0]), [1, hi - lo]])


def emit_quant(P, out_ap, in_ap, scr, n, pool_cols=0):
    vf, sf = P.ve, P.sf
    s_af = P.sem("s_af")
    s_vf = P.sem("s_vf")
    s_gp = P.sem("s_gp")
    nb = n // 16
    A = scr["A"][:, 0:n]
    Bt = scr["B"][:, 0:n]
    C = scr["C"][:, 0:n]
    mx = scr["mx"][:, 0:nb]
    sc = scr["sc"][:, 0:nb]
    rs = scr["rs"][:, 0:nb]
    in3 = in_ap.rearrange("p (b k) -> p b k", k=16)
    sf.wait(s_vf)
    sf.wait(s_gp)
    sf.activation(A, in_ap, AF.Abs)
    sf.activation(Bt, in_ap, AF.Sign)
    P.vf.tensor_reduce(mx, in3, AX.X, ALU.max, apply_absolute_value=True)
    P.vf.tensor_scalar(sc, mx, 1.0 / 6.0, 1e-12, ALU.mult, ALU.max)
    P.vf.reciprocal(rs, sc)
    vf.wait(s_af)
    nd = n - pool_cols
    assert nd % 16 == 0

    def chain(e, lo, hi):
        if hi <= lo:
            return
        b0, b1 = lo // 16, hi // 16
        Av = A[:, lo:hi].rearrange("p (b k) -> p b k", k=16)
        Cv = C[:, lo:hi]
        rsv = rs[:, b0:b1].unsqueeze(2).broadcast_to([128, b1 - b0, 16])
        scv = sc[:, b0:b1].unsqueeze(2).broadcast_to([128, b1 - b0, 16])
        e.tensor_tensor(Av, Av, rsv, ALU.mult)
        e.tensor_single_scalar(Cv, A[:, lo:hi], 0.25, ALU.is_gt)
        for m in (0.75, 1.25, 1.75):
            e.scalar_tensor_tensor(Cv, A[:, lo:hi], m, Cv, ALU.is_gt, ALU.add)
        e.tensor_single_scalar(Cv, Cv, 0.5, ALU.mult)
        for m in (2.5, 3.5, 5.0, 5.0):
            e.scalar_tensor_tensor(Cv, A[:, lo:hi], m, Cv, ALU.is_gt, ALU.add)
        e.tensor_tensor(Av, C[:, lo:hi].rearrange("p (b k) -> p b k", k=16),
                        scv, ALU.mult)
    if pool_cols > 0:
        P.ge.wait(s_af)
        P.ge.wait(s_vf)
        P.ge.wait(P.sem("s_pe"), P._quant_pe_guard) if False else None
        chain(P.ge, nd, n)
        P.ge.op("tensor_tensor", out_ap.tensor[:, 0:1] if False else None)             if False else None
        P.ge.tensor_tensor(
            out_ap if not isinstance(out_ap, bass.AP) else out_ap, None, None,
            None) if False else None
    chain(vf, 0, nd)
    if pool_cols > 0:
        P.vf.tensor_tensor(_apslice(out_ap, 0, nd), A[:, 0:nd], Bt[:, 0:nd],
                           ALU.mult)
        P.ge.tensor_tensor(_apslice(out_ap, nd, n), A[:, nd:n], Bt[:, nd:n],
                           ALU.mult, inc=s_gp)
    elif getattr(P, "_pool_finals", False):
        P.ve.op("tensor_copy", mx[:, 0:16], C[:, 0:16], inc=P.sem("s_ve"))
        P.ge.wait(P.sem("s_ve"))
        P.ge.wait(s_af)
        P.ge.tensor_tensor(A.rearrange("p (b k) -> p b k", k=16),
                           C.rearrange("p (b k) -> p b k", k=16),
                           sc.unsqueeze(2).broadcast_to([128, nb, 16]),
                           ALU.mult)
        P.ge.tensor_tensor(out_ap, A, Bt, ALU.mult, inc=s_gp)
    else:
        P.vf.tensor_tensor(out_ap, A, Bt, ALU.mult)


class QPrep:

    def __init__(self, P, jobs, d_prep):
        self.P = P
        self.d_prep = d_prep
        self.units = []
        for (w_f32, wq_bf, O, I) in jobs:
            npc = (I + 703) // 704
            base = I // npc
            assert base % 16 == 0 and base * npc == I
            for r in range(O // 128):
                for q in range(npc):
                    self.units.append((w_f32, wq_bf, r, q * base,
                                       (q + 1) * base))
        self.idx = 0
        self.qvf = [0, 0]
        self.qaf = [0, 0]
        self.loads = [0, 0]
        self.sdone = 0
        self.ld_idx = 0

    def alloc(self, ctx):
        nc = self.P.nc
        sbt = lambda n, s, d: ctx.enter_context(nc.sbuf_tensor(n, s, d))
        self.W = [sbt(f"qpW{i}", [128, 704], F32) for i in range(2)]
        self.WQ = sbt("qpWQ", [128, 704], BF16)
        self.scr = {
            "A": sbt("qpA", [128, 704], BF16),
            "B": sbt("qpB", [128, 704], BF16),
            "C": sbt("qpC", [128, 704], BF16),
            "mx": sbt("qpmx", [128, 44], F32),
            "sc": sbt("qpsc", [128, 44], F32),
            "rs": sbt("qprs", [128, 44], F32),
        }
        self.d_q = [self.P.sem("qp_dA"), self.P.sem("qp_dB")]

    def _load(self):
        if self.ld_idx >= len(self.units):
            return
        P = self.P
        w_f32, _, r, c0, c1 = self.units[self.ld_idx]
        b = self.ld_idx % 2
        P.sy.wait(P.sem("s_vf"), self.qvf[b])
        P.sy.wait(P.sem("s_af"), self.qaf[b])
        P.sy.dma(self.W[b][:, 0:c1 - c0],
                 w_f32[r * 128:(r + 1) * 128, c0:c1], inc=self.d_q[b])
        self.loads[b] = self.d_q[b].n
        self.ld_idx += 1

    def emit(self, k):
        P = self.P
        end = min(len(self.units), self.idx + k)
        while self.ld_idx < min(len(self.units), self.idx + 2):
            self._load()
        while self.idx < end:
            w_f32, wq_bf, r, c0, c1 = self.units[self.idx]
            b = self.idx % 2
            P.ve.wait(self.d_q[b], self.loads[b])
            P.se.wait(self.d_q[b], self.loads[b])
            P.ve.wait(self.d_prep, self.sdone)
            emit_quant(P, self.WQ[:, 0:c1 - c0], self.W[b][:, 0:c1 - c0],
                       self.scr, c1 - c0, pool_cols=0)
            self.qvf[b] = P.sem("s_vf").n
            self.qaf[b] = P.sem("s_af").n
            P.sy.wait(P.sem("s_vf"), self.qvf[b])
            P.sy.wait(P.sem("s_gp"))
            P.sy.dma(wq_bf[r * 128:(r + 1) * 128, c0:c1], self.WQ[:, 0:c1 - c0],
                     inc=self.d_prep)
            self.sdone = self.d_prep.n
            self.idx += 1
            self._load()

    def finish(self):
        self.emit(len(self.units))
        self.P.sy.wait(self.d_prep, self.sdone)


def emit_qprep(P, w_f32, wq_bf, O, I, scr, d_prep):
    sy = P.sy
    s_vf = P.sem("s_vf")
    s_af = P.sem("s_af")
    d_q = [P.sem("d_qpA"), P.sem("d_qpB")]
    Wt = [scr["W"], scr["W2"]]
    wq = scr["WQ"]
    st = P._qprep_state = getattr(P, "_qprep_state",
                                  {"qvf": [0, 0], "qaf": [0, 0],
                                   "loads": [0, 0], "sdone": 0})
    nR = O // 128

    def load(r):
        b = r % 2
        sy.wait(s_vf, st["qvf"][b])
        sy.wait(s_af, st["qaf"][b])
        sy.dma(Wt[b][:, 0:I], w_f32[r * 128:(r + 1) * 128, :], inc=d_q[b])
        st["loads"][b] = d_q[b].n

    load(0)
    for r in range(nR):
        b = r % 2
        if r + 1 < nR:
            load(r + 1)
        P.ve.wait(d_q[b], st["loads"][b])
        P.se.wait(d_q[b], st["loads"][b])
        P.ve.wait(d_prep, st["sdone"])
        emit_quant(P, wq[:, 0:I], Wt[b][:, 0:I], scr, I)
        st["qvf"][b] = P.sem("s_vf").n
        st["qaf"][b] = P.sem("s_af").n
        sy.wait(s_vf)
        sy.dma(wq_bf[r * 128:(r + 1) * 128, :], wq[:, 0:I], inc=d_prep)
        st["sdone"] = d_prep.n



def emit_ffn0(P, cs, xsrc, xdst, srcs, dr, pfx):
    te, ve, se, ge, sy = P.te, P.ve, P.se, P.ge, P.sy
    nc = P.nc
    II = 2816
    NIC = II // 128
    phase_ctx = ExitStack()

    def sb(name, shape, dtype):
        return phase_ctx.enter_context(nc.sbuf_tensor(pfx + name, shape, dtype))

    def ps(name, shape, dtype=F32):
        return phase_ctx.enter_context(nc.psum_tensor(pfx + name, shape, dtype))

    d_w = P.sem(pfx + "d_w")
    d_x = [P.sem(pfx + f"d_x{i}") for i in range(1)]
    d_out = P.sem("d_out")
    s_pe = P.sem("s_pe")
    s_ae = P.sem("s_ae")
    s_ve = P.sem("s_ve")
    s_vf = P.sem("s_vf")
    s_af = P.sem("s_af")

    wgT = sb("wgT", [128, DC * II], BF16)
    wuT = sb("wuT", [128, DC * II], BF16)
    wdT = sb("wdT", [128, NIC * 1024], BF16)
    x_tm2 = [sb(f"xtm{i}", [128, 4 * 1024], BF16) for i in range(2)]
    xq_fm = sb("xqfm", [128, DC * QS], BF16)
    g_sb = sb("g", [128, II], BF16)
    h_sb = sb("h", [128, II], BF16)
    hq_sb = sb("hq", [128, II], BF16)
    hqT = sb("hqT", [128, II], BF16)
    scr = {
        "A": sb("scrA", [128, II], BF16),
        "B": sb("scrB", [128, II], BF16),
        "C": sb("scrC", [128, II], BF16),
        "mx": sb("scrmx", [128, 176], F32),
        "sc": sb("scrsc", [128, 176], F32),
        "rs": sb("scrrs", [128, 176], F32),
    }
    y_sb = sb("y", [128, 1024], F32)
    st8 = sb("st8", [128, 8], F32)
    x2t = sb("x2t", [128, 1024], BF16)
    yjunk = x2t
    pA = Bank(ps("pA", [128, 512], F32).ap())
    pB = Bank(ps("pB", [128, 512], F32).ap())
    pT = [Bank(ps(f"pT{i}", [128, 1024], BF16).ap()) for i in range(2)]
    o_ps2 = [Bank(ps(f"o{i}", [128, 1024], F32).ap()) for i in range(2)]

    prep4 = getattr(P, "_prep4", None)
    if prep4 is not None:
        sy.wait(P.sem("d_cast"), P._prep4_c4)
        prep4.alloc(phase_ctx)
    P._pool_finals = True

    for c in range(DC):
        sy.dma(wgT[:, c * II:(c + 1) * II], srcs["gate"][:, c * 128:(c + 1) * 128],
               inc=d_w, transpose=True)
        sy.dma(wuT[:, c * II:(c + 1) * II], srcs["up"][:, c * 128:(c + 1) * 128],
               inc=d_w, transpose=True)
    for c in range(NIC):
        sy.dma(wdT[:, c * 1024:(c + 1) * 1024],
               srcs["down"][:, c * 128:(c + 1) * 128], inc=d_w, transpose=True)
    te.wait(d_w)

    x_load = [0]
    xt_done = [0]
    ht_done = [0]
    down_done = {}
    ident = cs["ident"]

    def emit_deferred_ln(gj):
        g2, j2 = divmod(gj, 4)
        o_ps = o_ps2[gj % 2]
        ve.wait(s_pe, down_done[gj])
        ve.wait(d_out)
        emit_ln(P, x2t[:, :], o_ps.ap[:, :],
                x_tm2[g2 % 2][:, j2 * 1024:(j2 + 1) * 1024],
                y_sb, yjunk, st8, inc=s_ve)
        o_ps.release(s_ve)
        sy.wait(s_ve)
        sy.dma(xdst[g2 * QS + j2 * 128:g2 * QS + (j2 + 1) * 128, :],
               x2t[:, :], inc=d_out)

    for g in range(S // QS):
        sy.wait(s_vf)
        sy.wait(s_af)
        sy.wait(s_ve)
        src2 = xsrc[g * QS:(g + 1) * QS, :].rearrange("(t p) d -> p t d", p=128)
        sy.dma(x_tm2[g % 2][:, :].rearrange("p (t d) -> p t d", t=4), src2,
               inc=d_x[0])
        x_load[0] = d_x[0].n
        P.ve.wait(d_x[0], x_load[0])
        P.se.wait(d_x[0], x_load[0])
        if prep4 is not None:
            prep4.emit(13)
        for j in range(4):
            P.ve.wait(s_pe, ht_done[0])
            emit_quant(P, hq_sb[:, 0:1024],
                       x_tm2[g % 2][:, j * 1024:(j + 1) * 1024], scr, 1024,
                       pool_cols=0)
            bk = pT[0]
            bk.acquire(te)
            te.wait(P.sem("s_vf"))
            te.wait(P.sem("s_gp"))
            for c in range(DC):
                ins = te.op("transpose", bk.ap[:, c * 128:(c + 1) * 128],
                            hq_sb[:, c * 128:(c + 1) * 128], ident[:, :],
                            inc=s_pe if c == DC - 1 else None)
            xt_done[0] = s_pe.n
            se.wait(s_pe)
            for half in range(2):
                se.activation(
                    xq_fm[:, :].rearrange("p (c t) -> p c t", c=DC)[
                        :, half * 4:(half + 1) * 4, j * 128:(j + 1) * 128],
                    bk.ap[:, half * 512:(half + 1) * 512].rearrange(
                        "p (c t) -> p c t", c=4),
                    AF.Copy, inc=s_ae)
            bk.release(s_ae)
            te.wait(s_ae)
            nch = (II + 511) // 512
            for ch in range(nch):
                w0 = ch * 512
                w1 = min(II, w0 + 512)
                for (wT, bank) in ((wgT, pA), (wuT, pB)):
                    bank.acquire(te)
                    for ic in range(DC):
                        te.matmul(bank.ap[:, 0:w1 - w0],
                                  xq_fm[:, ic * QS + j * 128:
                                        ic * QS + (j + 1) * 128],
                                  wT[:, ic * II + w0:ic * II + w1],
                                  start=(ic == 0), stop=(ic == DC - 1),
                                  inc=s_pe if ic == DC - 1 else None)
                se.wait(s_pe)
                if ch == 0:
                    se.wait(s_vf)
                se.activation(g_sb[:, w0:w1], pA.ap[:, 0:w1 - w0], AF.Silu,
                              inc=s_ae)
                pA.release(s_ae)
                se.activation(h_sb[:, w0:w1], pB.ap[:, 0:w1 - w0], AF.Copy,
                              inc=s_ae)
                pB.release(s_ae)
            ve.wait(s_ae)
            P.vf.tensor_tensor(h_sb[:, :], g_sb[:, :], h_sb[:, :], ALU.mult)
            P.ve.wait(s_pe, xt_done[0])
            emit_quant(P, hq_sb[:, :], h_sb[:, :], scr, II, pool_cols=0)
            for blk in range(3):
                c0 = blk * 8
                c1 = min(NIC, c0 + 8)
                bk = pT[blk % 2]
                bk.acquire(te)
                if blk == 0:
                    te.wait(P.sem("s_vf"))
                    te.wait(P.sem("s_gp"))
                for c in range(c0, c1):
                    te.op("transpose", bk.ap[:, (c - c0) * 128:(c - c0 + 1) * 128],
                          hq_sb[:, c * 128:(c + 1) * 128], ident[:, :],
                          inc=s_pe if c == c1 - 1 else None)
                ht_done[0] = s_pe.n
                se.wait(s_pe)
                se.activation(hqT[:, c0 * 128:c1 * 128],
                              bk.ap[:, 0:(c1 - c0) * 128], AF.Copy, inc=s_ae)
                bk.release(s_ae)
            gj = g * 4 + j
            o_ps = o_ps2[gj % 2]
            o_ps.acquire(te)
            te.wait(s_ae)
            for half in range(2):
                for ic in range(NIC):
                    te.matmul(o_ps.ap[:, half * 512:(half + 1) * 512],
                              hqT[:, ic * 128:(ic + 1) * 128],
                              wdT[:, ic * 1024 + half * 512:
                                  ic * 1024 + (half + 1) * 512],
                              start=(ic == 0), stop=(ic == NIC - 1),
                              inc=s_pe if ic == NIC - 1 else None)
            down_done[gj] = s_pe.n
            if gj > 0:
                emit_deferred_ln(gj - 1)
    emit_deferred_ln(S // QS * 4 - 1)
    P._pool_finals = False
    if prep4 is not None:
        prep4.finish()
    sy.wait(d_out)
    P.barrier()
    phase_ctx.close()



class PrepJobs:

    def __init__(self, P, srcs, dr, pfx):
        self.P = P
        self.pfx = pfx
        self.jobs = []
        HID = 4096
        for nm, dst, O in (("h1", "h1wT", HID), ("h2", "h2wT", HID),
                           ("w3", "w3T", 1024)):
            for ic in range(HID // 128):
                self.jobs.append((srcs[nm], dr[dst], O, ic))
        self.idx = 0
        self.st_done = [0, 0]
        self.wt = None

    def alloc(self, ctx):
        nc = self.P.nc
        t = ctx.enter_context(
            nc.sbuf_tensor(self.pfx + "wprep0", [128, 2048], BF16))
        self.wt = [t, t]
        self.d_p = [self.P.sem(self.pfx + "d_p0")] * 2

    def emit(self, k):
        sy = self.P.sy
        end = min(len(self.jobs), self.idx + k)
        while self.idx < end:
            srcap, dst, O, ic = self.jobs[self.idx]
            b = self.idx % 2
            for o0 in range(0, O, 2048):
                o1 = min(O, o0 + 2048)
                sy.wait(self.d_p[b], self.st_done[b])
                sy.dma(self.wt[b][:, 0:o1 - o0],
                       srcap[o0:o1, ic * 128:(ic + 1) * 128],
                       inc=self.d_p[b], transpose=True)
                sy.wait(self.d_p[b], self.d_p[b].n)
                sy.dma(dst[ic, :, o0:o1], self.wt[b][:, 0:o1 - o0],
                       inc=self.d_p[b])
                self.st_done[b] = self.d_p[b].n
            self.idx += 1

    def finish(self):
        self.emit(len(self.jobs))
        self.P.sy.wait(self.d_p[0], self.st_done[0])
        self.P.sy.wait(self.d_p[1], self.st_done[1])


def emit_ffn1(P, cs, xsrc, ydst, srcs, dr, pfx):
    te, ve, se, ge, sy = P.te, P.ve, P.se, P.ge, P.sy
    nc = P.nc
    HID = 4096
    NHC = HID // 128
    phase_ctx = ExitStack()

    def sb(name, shape, dtype):
        return phase_ctx.enter_context(nc.sbuf_tensor(pfx + name, shape, dtype))

    def ps(name, shape, dtype=F32):
        return phase_ctx.enter_context(nc.psum_tensor(pfx + name, shape, dtype))

    d_w = P.sem(pfx + "d_w")
    d_x = [P.sem(pfx + f"d_x{i}") for i in range(2)]
    d_h = [P.sem(pfx + f"d_h{i}") for i in range(2)]
    d_s = P.sem(pfx + "d_s")
    d_out = P.sem("d_out")
    s_pe = P.sem("s_pe")
    s_ae = P.sem("s_ae")
    s_ve = P.sem("s_ve")

    sc_all = sb("scales", [128, 4 * 32 + 8], F32)
    s1v = sc_all[:, 0:32]
    s2v = sc_all[:, 32:64]
    sh1v = sc_all[:, 64:96]
    sh2v = sc_all[:, 96:128]
    for i, nm in enumerate(("s1", "s2", "sh1", "sh2")):
        sy.dma(sc_all[:, i * 32:(i + 1) * 32],
               srcs[nm].ap().rearrange("(c p) -> p c", p=128), inc=d_s,
               allow_slow_non_contiguous=True)
    s3row = sb("s3row", [1, 1024], F32)
    s3rowb = sb("s3rowb", [1, 1024], BF16)
    s3b = sb("s3b", [128, 1024], BF16)
    sy.dma(s3row[0:1, :], srcs["s3"].rearrange("(a d) -> a d", a=1), inc=d_s)
    se.wait(d_s)
    se.activation(s3rowb[0:1, :], s3row[0:1, :], AF.Copy, inc=s_ae)
    te.wait(s_ae)

    stA = ExitStack()

    def sbA(name, shape, dtype):
        return stA.enter_context(nc.sbuf_tensor(pfx + name, shape, dtype))

    w1T = sbA("w1T", [128, DC * HID], BF16)
    w2T = sbA("w2T", [128, DC * HID], BF16)
    x_fm = [sbA(f"xfm{i}", [128, DC * QS], BF16) for i in range(2)]
    g_sb = [sbA(f"gA{i}", [128, 512], BF16) for i in range(2)]
    u_sb = [sbA(f"uA{i}", [128, 512], BF16) for i in range(2)]
    h1blk = sbA("h1blk", [128, NHC * QS], BF16)
    pAb = [Bank(stA.enter_context(
        nc.psum_tensor(pfx + f"pA{i}", [128, 512], F32)).ap()) for i in range(2)]
    pBb = [Bank(stA.enter_context(
        nc.psum_tensor(pfx + f"pB{i}", [128, 512], F32)).ap()) for i in range(2)]
    for c in range(DC):
        sy.dma(w1T[:, c * HID:(c + 1) * HID],
               srcs["w1"][:, c * 128:(c + 1) * 128], inc=d_w, transpose=True)
        sy.dma(w2T[:, c * HID:(c + 1) * HID],
               srcs["w2"][:, c * 128:(c + 1) * 128], inc=d_w, transpose=True)
    te.wait(d_w)
    ve.wait(d_s)
    x_loads = [0, 0]
    for tb in range(S // QS):
        b = tb % 2
        if tb >= 2:
            sy.wait(s_pe)
        for c in range(DC):
            sy.dma(x_fm[b][:, c * QS:(c + 1) * QS],
                   xsrc[tb * QS:(tb + 1) * QS, c * 128:(c + 1) * 128],
                   inc=d_x[b], transpose=True)
        x_loads[b] = d_x[b].n
        te.wait(d_x[b], x_loads[b])
        mul_done = {}
        for hc in range(NHC):
            pA = pAb[hc % 2]
            pB = pBb[hc % 2]
            for (wT, bank) in ((w1T, pA), (w2T, pB)):
                bank.acquire(te)
                for ic in range(DC):
                    te.matmul(bank.ap,
                              wT[:, ic * HID + hc * 128:ic * HID + (hc + 1) * 128],
                              x_fm[b][:, ic * QS:(ic + 1) * QS],
                              start=(ic == 0), stop=(ic == DC - 1),
                              inc=s_pe if ic == DC - 1 else None)
            se.wait(s_pe)
            if hc >= 2:
                se.wait(P.sem("s_vf"), mul_done[hc - 2])
            se.activation(g_sb[hc % 2][:, :], pA.ap, AF.Silu,
                          scale=s1v[:, hc:hc + 1], inc=s_ae)
            pA.release(s_ae)
            se.activation(u_sb[hc % 2][:, :], pB.ap, AF.Copy,
                          scale=s2v[:, hc:hc + 1], inc=s_ae)
            pB.release(s_ae)
            ve.wait(s_ae)
            P.vf.tensor_tensor(h1blk[:, hc * QS:(hc + 1) * QS],
                               g_sb[hc % 2][:, :], u_sb[hc % 2][:, :], ALU.mult)
            mul_done[hc] = P.sem("s_vf").n
        sy.wait(P.sem("s_vf"))
        sy.dma(dr["h1s"][:, :, tb * QS:(tb + 1) * QS].rearrange(
            "c p t -> p c t"),
            h1blk[:, :].rearrange("p (c t) -> p c t", c=NHC), inc=d_h[0])
    sy.wait(d_h[0])
    P.barrier()
    stA.close()

    def big_stage(hin, hout, wtiled, scalev, TSUP, store):
        stB = ExitStack()

        def sbB(name, shape, dtype):
            return stB.enter_context(nc.sbuf_tensor(pfx + "B" + name, shape, dtype))

        nt = TSUP // 512
        hi = sbB("hi", [128, NHC * TSUP], BF16)
        ho = sbB("ho", [128, NHC * TSUP], BF16)
        wch = [sbB(f"wch{i}", [128, NHC * 256], BF16) for i in range(2)]
        pC = [Bank(stB.enter_context(
            nc.psum_tensor(pfx + f"pC{store}{i}", [128, 512], F32)).ap())
            for i in range(2)]
        d_hi = P.sem(pfx + "d_hi" + store)
        d_wc = [P.sem(pfx + f"d_wc{store}{i}") for i in range(2)]
        wc_loads = [0, 0]
        hi_loads = 0
        jo_pe_done_prev = [None]
        for tsup in range(S // TSUP):
            sy.wait(s_pe)
            sy.dma(hi[:, :].rearrange("p (c t) -> p c t", c=NHC),
                   hin[:, :, tsup * TSUP:(tsup + 1) * TSUP].rearrange(
                       "c p t -> p c t"), inc=d_hi)
            hi_loads = d_hi.n
            te.wait(d_hi, hi_loads)
            se.wait(d_hi, hi_loads)
            njo = HID // 256

            def ld_wch(jo, wait_pe):
                wb = jo % 2
                if wait_pe is not None:
                    sy.wait(s_pe, wait_pe)
                sy.dma(wch[wb][:, :].rearrange("p (c o) -> p c o", c=NHC),
                       wtiled[:, :, jo * 256:(jo + 1) * 256].rearrange(
                           "c p o -> p c o"), inc=d_wc[wb])
                wc_loads[wb] = d_wc[wb].n

            jo_pe_done = {}
            ld_wch(0, jo_pe_done_prev[0])
            ld_wch(1, jo_pe_done_prev[0])
            for jo in range(njo):
                wb = jo % 2
                te.wait(d_wc[wb], wc_loads[wb])
                for jt in range(2):
                    o0 = jo * 256 + jt * 128
                    for th in range(nt):
                        bank = pC[(jt * nt + th) % 2]
                        bank.acquire(te)
                        for ic in range(NHC):
                            te.matmul(
                                bank.ap,
                                wch[wb][:, ic * 256 + jt * 128:
                                        ic * 256 + (jt + 1) * 128],
                                hi[:, ic * TSUP + th * 512:
                                   ic * TSUP + (th + 1) * 512],
                                start=(ic == 0), stop=(ic == NHC - 1),
                                inc=s_pe if ic == NHC - 1 else None)
                        se.wait(s_pe)
                        se.activation(
                            ho[:, (o0 // 128) * TSUP + th * 512:
                               (o0 // 128) * TSUP + (th + 1) * 512],
                            bank.ap, AF.Gelu,
                            scale=scalev[:, o0 // 128:o0 // 128 + 1],
                            inc=s_ae)
                        bank.release(s_ae)
                jo_pe_done[jo] = s_pe.n
                if jo + 2 < njo:
                    ld_wch(jo + 2, jo_pe_done[jo])
            jo_pe_done_prev[0] = s_pe.n
            sy.wait(s_ae)
            sy.dma(hout[:, :, tsup * TSUP:(tsup + 1) * TSUP].rearrange(
                "c p t -> p c t"),
                ho[:, :].rearrange("p (c t) -> p c t", c=NHC), inc=d_hi)
        sy.wait(d_hi)
        P.barrier()
        stB.close()

    big_stage(dr["h1s"], dr["h2s"], dr["h1wT"], sh1v, 1024, "B")

    stC = ExitStack()

    def sbC(name, shape, dtype):
        return stC.enter_context(nc.sbuf_tensor(pfx + "C" + name, shape, dtype))

    TS2 = 512
    hi = sbC("hi", [128, NHC * TS2], BF16)
    h3 = sbC("h3", [128, NHC * TS2], BF16)
    wch = [sbC(f"wch{i}", [128, NHC * 256], BF16) for i in range(2)]
    w3T = sbC("w3T", [128, NHC * 1024], BF16)
    res_tm = [sbC(f"res{i}", [128, 1024], BF16) for i in range(2)]
    y_sb = sbC("y", [128, 1024], F32)
    yjunk = sbC("yj", [128, 1024], BF16)
    st8 = sbC("st8", [128, 8], F32)
    yout = [sbC(f"yo{i}", [128, 1024], F32) for i in range(2)]
    pC = [Bank(stC.enter_context(
        nc.psum_tensor(pfx + f"pD{i}", [128, 512], F32)).ap()) for i in range(2)]
    o_ps = Bank(stC.enter_context(
        nc.psum_tensor(pfx + "oD", [128, 1024], F32)).ap())
    psb = Bank(stC.enter_context(
        nc.psum_tensor(pfx + "psb", [128, 1024], F32)).ap())

    d_hi = P.sem(pfx + "d_hiC")
    d_wc = [P.sem(pfx + f"d_wcC{i}") for i in range(2)]
    d_w3 = P.sem(pfx + "d_w3")
    d_r = [P.sem(pfx + f"d_rC{i}") for i in range(2)]

    psb.acquire(te)
    for half in range(2):
        te.matmul(psb.ap[:, half * 512:(half + 1) * 512],
                  cs["ones1"][0:1, 0:128], s3rowb[0:1, half * 512:(half + 1) * 512],
                  start=True, stop=True, inc=s_pe)
    se.wait(s_pe)
    se.activation(s3b[:, :], psb.ap[:, :], AF.Copy, inc=s_ae)
    psb.release(s_ae)

    sy.dma(w3T[:, :].rearrange("p (c o) -> p c o", c=NHC),
           dr["w3T"][:, :, :].rearrange("c p o -> p c o"), inc=d_w3)
    te.wait(d_w3)
    wc_loads = [0, 0]
    r_loads = [0, 0]
    jo_pe_prev = [None]
    for tsup in range(S // TS2):
        sy.wait(s_pe)
        sy.dma(hi[:, :].rearrange("p (c t) -> p c t", c=NHC),
               dr["h2s"][:, :, tsup * TS2:(tsup + 1) * TS2].rearrange(
                   "c p t -> p c t"), inc=d_hi)
        te.wait(d_hi)
        njo = HID // 256

        def ld_wch2(jo, wait_pe):
            wb = jo % 2
            if wait_pe is not None:
                sy.wait(s_pe, wait_pe)
            sy.dma(wch[wb][:, :].rearrange("p (c o) -> p c o", c=NHC),
                   dr["h2wT"][:, :, jo * 256:(jo + 1) * 256].rearrange(
                       "c p o -> p c o"), inc=d_wc[wb])
            wc_loads[wb] = d_wc[wb].n

        jo_pe = {}
        ld_wch2(0, jo_pe_prev[0])
        ld_wch2(1, jo_pe_prev[0])
        for jo in range(njo):
            wb = jo % 2
            te.wait(d_wc[wb], wc_loads[wb])
            for jt in range(2):
                o0c = jo * 2 + jt
                bank = pC[o0c % 2]
                bank.acquire(te)
                for ic in range(NHC):
                    te.matmul(bank.ap,
                              wch[wb][:, ic * 256 + jt * 128:
                                      ic * 256 + (jt + 1) * 128],
                              hi[:, ic * TS2:(ic + 1) * TS2],
                              start=(ic == 0), stop=(ic == NHC - 1),
                              inc=s_pe if ic == NHC - 1 else None)
                se.wait(s_pe)
                se.activation(h3[:, o0c * TS2:(o0c + 1) * TS2], bank.ap,
                              AF.Gelu, scale=sh2v[:, o0c:o0c + 1], inc=s_ae)
                bank.release(s_ae)
            jo_pe[jo] = s_pe.n
            if jo + 2 < njo:
                ld_wch2(jo + 2, jo_pe[jo])
        jo_pe_prev[0] = s_pe.n
        te.wait(s_ae)
        for j in range(4):
            rb = (tsup * 4 + j) % 2
            sy.wait(s_ve)
            sy.dma(res_tm[rb][:, :],
                   xsrc[tsup * TS2 + j * 128:tsup * TS2 + (j + 1) * 128, :],
                   inc=d_r[rb])
            r_loads[rb] = d_r[rb].n
            o_ps.acquire(te)
            for half in range(2):
                for ic in range(NHC):
                    te.matmul(o_ps.ap[:, half * 512:(half + 1) * 512],
                              h3[:, ic * TS2 + j * 128:ic * TS2 + (j + 1) * 128],
                              w3T[:, ic * 1024 + half * 512:
                                  ic * 1024 + (half + 1) * 512],
                              start=(ic == 0), stop=(ic == NHC - 1),
                              inc=s_pe if ic == NHC - 1 else None)
            ve.wait(s_pe)
            ve.wait(d_r[rb], r_loads[rb])
            ve.wait(s_ae)
            ve.wait(d_out)
            emit_ln(P, yout[rb][:, :], o_ps.ap[:, :], res_tm[rb][:, :],
                    y_sb, yjunk, st8, inc=s_ve, premul=s3b[:, :])
            o_ps.release(s_ve)
            sy.wait(s_ve)
            sy.dma(ydst[tsup * TS2 + j * 128:tsup * TS2 + (j + 1) * 128, :],
                   yout[rb][:, :], inc=d_out)
    sy.wait(d_out)
    P.barrier()
    stC.close()
    phase_ctx.close()


def emit_cast(P, dst, src, rows, chunk=4096):
    d_cast = P.sem("d_cast")
    for r0 in range(0, rows, chunk):
        r1 = min(rows, r0 + chunk)
        P.ge.dma(dst[r0:r1, :], src[r0:r1, :], inc=d_cast)
    return d_cast


def build_launch_a():
    P = Prog("mkA")
    x = P.dram("x", [S, D], F32, kind="ExternalInput")
    wqkv = P.dram("wqkv", [3072, D], F32, kind="ExternalInput")
    wo = P.dram("wo", [D, D], F32, kind="ExternalInput")
    xs1 = P.dram("xs1", [S, D], BF16, kind="ExternalOutput")
    xs0 = P.dram("xs0", [S, D], BF16)
    bqkv = P.dram("bqkv", [3072, D], BF16)
    bwo = P.dram("bwo", [D, D], BF16)
    cs = emit_consts(P)
    d_cast = emit_cast(P, xs0, x, S)
    emit_cast(P, bqkv, wqkv, 3072)
    emit_cast(P, bwo, wo, D)
    P.sy.wait(d_cast)
    emit_attn(P, cs, xs0, xs1, {"qkv": bqkv, "o": bwo}, "l0", "p1_")
    return P.realize()


def build_launch_b():
    P = Prog("mkB")
    xs1 = P.dram("xs1", [S, D], BF16, kind="ExternalInput")
    wg = P.dram("wg", [2816, D], F32, kind="ExternalInput")
    wu = P.dram("wu", [2816, D], F32, kind="ExternalInput")
    wd = P.dram("wd", [D, 2816], F32, kind="ExternalInput")
    xs2 = P.dram("xs2", [S, D], BF16, kind="ExternalOutput")
    bg = P.dram("bg", [2816, D], BF16)
    bu = P.dram("bu", [2816, D], BF16)
    bd = P.dram("bd", [D, 2816], BF16)
    xq_s = P.dram("xq_s", [S, D], BF16)
    hq_s = P.dram("hq_s", [S, 2816], BF16)
    cs = emit_consts(P)
    d_prep = P.sem("d_prep")
    qctx = ExitStack()
    nc = P.nc
    scr = {}
    for nm, shape, dt_ in (("W", [128, 2816], F32), ("W2", [128, 2816], F32),
                           ("WQ", [128, 2816], BF16),
                           ("A", [128, 2816], BF16), ("B", [128, 2816], BF16),
                           ("C", [128, 2816], BF16), ("D", [128, 2816], BF16),
                           ("E", [128, 2816], BF16), ("mx", [128, 176], F32),
                           ("sc", [128, 176], F32), ("rs", [128, 176], F32)):
        scr[nm] = qctx.enter_context(nc.sbuf_tensor("qp_" + nm, shape, dt_))
    emit_qprep(P, wg, bg, 2816, 1024, scr, d_prep)
    emit_qprep(P, wu, bu, 2816, 1024, scr, d_prep)
    emit_qprep(P, wd, bd, 1024, 2816, scr, d_prep)
    P.sy.wait(d_prep)
    P.barrier()
    qctx.close()
    emit_ffn0(P, cs, xs1, xs2, {"gate": bg, "up": bu, "down": bd}, {}, "p2_")
    return P.realize()


def build_launch_c():
    P = Prog("mkC")
    xs2 = P.dram("xs2", [S, D], BF16, kind="ExternalInput")
    ws = {nm: P.dram("w" + nm, [D, D], F32, kind="ExternalInput")
          for nm in ("q", "k", "v", "o")}
    xs3 = P.dram("xs3", [S, D], BF16, kind="ExternalOutput")
    bs = {nm: P.dram("b" + nm, [D, D], BF16) for nm in ("q", "k", "v", "o")}
    cs = emit_consts(P)
    d_cast = None
    for nm in ("q", "k", "v", "o"):
        d_cast = emit_cast(P, bs[nm], ws[nm], D)
    P.sy.wait(d_cast)
    emit_attn(P, cs, xs2, xs3, bs, "l1", "p3_")
    return P.realize()


def build_launch_d():
    P = Prog("mkD")
    HID = 4096
    xs3 = P.dram("xs3", [S, D], BF16, kind="ExternalInput")
    y = P.dram("y", [S, D], F32, kind="ExternalOutput")
    srcs = {}
    fsrc = {}
    for nm, shape in (("w1", [HID, D]), ("w2", [HID, D]), ("h1", [HID, HID]),
                      ("h2", [HID, HID]), ("w3", [D, HID])):
        fsrc[nm] = P.dram("f_" + nm, shape, F32, kind="ExternalInput")
        srcs[nm] = P.dram("b_" + nm, shape, BF16)
    for nm, n in (("s1", HID), ("s2", HID), ("sh1", HID), ("sh2", HID),
                  ("s3", D)):
        srcs[nm] = P.dram("v_" + nm, [n], F32, kind="ExternalInput")
    dr = {
        "h1s": P.dram("h1s", [32, 128, S], BF16),
        "h2s": P.dram("h2s", [32, 128, S], BF16),
        "h1wT": P.dram("h1wT", [32, 128, HID], BF16),
        "h2wT": P.dram("h2wT", [32, 128, HID], BF16),
        "w3T": P.dram("w3T", [32, 128, D], BF16),
    }
    cs = emit_consts(P)
    d_cast = None
    for nm in ("w1", "w2", "h1", "h2", "w3"):
        d_cast = emit_cast(P, srcs[nm], fsrc[nm], fsrc[nm].shape[0])
    P.sy.wait(d_cast)
    emit_ffn1(P, cs, xs3, y, srcs, dr, "p4_")
    return P.realize()


def kernel_unfused(L0_attn_x, L0_attn_W_qkv, L0_attn_W_o, L0_ffn_w_gate, L0_ffn_w_up,
           L0_ffn_w_down, L1_attn_Wq, L1_attn_Wk, L1_attn_Wv, L1_attn_Wo,
           L1_ffn_w1_w, L1_ffn_w1_s, L1_ffn_w2_w, L1_ffn_w2_s, L1_ffn_h1_w,
           L1_ffn_h1_s, L1_ffn_h2_w, L1_ffn_h2_s, L1_ffn_w3_w, L1_ffn_w3_s):
    n = 8
    consts = make_consts()
    x = np.ascontiguousarray(np.asarray(L0_attn_x, dtype=np.float32))
    cores = list(range(n))

    def rep(arr):
        a = np.ascontiguousarray(np.asarray(arr, dtype=np.float32))
        return [a] * n

    nc_a = build_launch_a()
    in_a = [dict(x=x[b], wqkv=rep(L0_attn_W_qkv)[b], wo=rep(L0_attn_W_o)[b],
                 **consts) for b in range(n)]
    res_a = run_bass_kernel_spmd(nc_a, in_a, core_ids=cores)
    xs1 = [res_a.results[b]["xs1"] for b in range(n)]

    nc_b = build_launch_b()
    in_b = [dict(xs1=xs1[b], wg=rep(L0_ffn_w_gate)[b], wu=rep(L0_ffn_w_up)[b],
                 wd=rep(L0_ffn_w_down)[b], **consts) for b in range(n)]
    res_b = run_bass_kernel_spmd(nc_b, in_b, core_ids=cores)
    xs2 = [res_b.results[b]["xs2"] for b in range(n)]

    nc_c = build_launch_c()
    in_c = [dict(xs2=xs2[b], wq=rep(L1_attn_Wq)[b], wk=rep(L1_attn_Wk)[b],
                 wv=rep(L1_attn_Wv)[b], wo=rep(L1_attn_Wo)[b], **consts)
            for b in range(n)]
    res_c = run_bass_kernel_spmd(nc_c, in_c, core_ids=cores)
    xs3 = [res_c.results[b]["xs3"] for b in range(n)]

    nc_d = build_launch_d()
    in_d = [dict(xs3=xs3[b], f_w1=rep(L1_ffn_w1_w)[b], f_w2=rep(L1_ffn_w2_w)[b],
                 f_h1=rep(L1_ffn_h1_w)[b], f_h2=rep(L1_ffn_h2_w)[b],
                 f_w3=rep(L1_ffn_w3_w)[b], v_s1=rep(L1_ffn_w1_s)[b],
                 v_s2=rep(L1_ffn_w2_s)[b], v_sh1=rep(L1_ffn_h1_s)[b],
                 v_sh2=rep(L1_ffn_h2_s)[b], v_s3=rep(L1_ffn_w3_s)[b],
                 **consts) for b in range(n)]
    res_d = run_bass_kernel_spmd(nc_d, in_d, core_ids=cores)
    y = np.stack([np.asarray(res_d.results[b]["y"]) for b in range(n)], axis=0)
    return y


def build_fused():
    P = Prog("mk")
    HID = 4096
    x = P.dram("x", [S, D], F32, kind="ExternalInput")
    y = P.dram("y", [S, D], F32, kind="ExternalOutput")
    wqkv = P.dram("wqkv", [3072, D], F32, kind="ExternalInput")
    wo0 = P.dram("wo0", [D, D], F32, kind="ExternalInput")
    wg = P.dram("wg", [2816, D], F32, kind="ExternalInput")
    wu = P.dram("wu", [2816, D], F32, kind="ExternalInput")
    wd = P.dram("wd", [D, 2816], F32, kind="ExternalInput")
    w1a = {nm: P.dram("w" + nm, [D, D], F32, kind="ExternalInput")
           for nm in ("q", "k", "v", "o")}
    fsrc = {}
    srcs4 = {}
    for nm, shape in (("w1", [HID, D]), ("w2", [HID, D]), ("h1", [HID, HID]),
                      ("h2", [HID, HID]), ("w3", [D, HID])):
        fsrc[nm] = P.dram("f_" + nm, shape, F32, kind="ExternalInput")
        srcs4[nm] = P.dram("b_" + nm, shape, BF16)
    for nm, nn in (("s1", HID), ("s2", HID), ("sh1", HID), ("sh2", HID),
                   ("s3", D)):
        srcs4[nm] = P.dram("v_" + nm, [nn], F32, kind="ExternalInput")
    xs0 = P.dram("xs0", [S, D], BF16)
    xs1 = P.dram("xs1", [S, D], BF16)
    xs2 = P.dram("xs2", [S, D], BF16)
    xs3 = P.dram("xs3", [S, D], BF16)
    bqkv = P.dram("bqkv", [3072, D], BF16)
    bwo0 = P.dram("bwo0", [D, D], BF16)
    bg = P.dram("bg", [2816, D], BF16)
    bu = P.dram("bu", [2816, D], BF16)
    bd = P.dram("bd", [D, 2816], BF16)
    b1a = {nm: P.dram("b1" + nm, [D, D], BF16) for nm in ("q", "k", "v", "o")}
    dr4 = {
        "h1s": P.dram("h1s", [32, 128, S], BF16),
        "h2s": P.dram("h2s", [32, 128, S], BF16),
        "h1wT": P.dram("h1wT", [32, 128, HID], BF16),
        "h2wT": P.dram("h2wT", [32, 128, HID], BF16),
        "w3T": P.dram("w3T", [32, 128, D], BF16),
    }

    cs = emit_consts(P)
    d_cast = emit_cast(P, bqkv, wqkv, 3072)
    emit_cast(P, bwo0, wo0, D)
    emit_cast(P, xs0, x, S, chunk=1024)
    c1 = d_cast.n
    for nm in ("q", "k", "v", "o"):
        emit_cast(P, b1a[nm], w1a[nm], D)
    c3 = d_cast.n
    for nm in ("w1", "w2", "h1", "h2", "w3"):
        emit_cast(P, srcs4[nm], fsrc[nm], fsrc[nm].shape[0])
    c4 = d_cast.n

    d_prep = P.sem("d_prep")
    qp = QPrep(P, [(wg, bg, 2816, 1024), (wu, bu, 2816, 1024),
                   (wd, bd, 1024, 2816)], d_prep)
    P._p1_qprep = qp
    P.sy.wait(d_cast, c1)
    emit_attn(P, cs, xs0, xs1, {"qkv": bqkv, "o": bwo0}, "l0", "p1_")
    qp.finish()

    prep4 = PrepJobs(P, srcs4, dr4, "p4p_")
    P._prep4 = prep4
    P._prep4_c4 = c4
    emit_ffn0(P, cs, xs1, xs2, {"gate": bg, "up": bu, "down": bd}, {}, "p2_")

    P.sy.wait(d_cast, c3)
    emit_attn(P, cs, xs2, xs3, b1a, "l1", "p3_")

    prep4.finish()
    P.barrier()
    emit_ffn1(P, cs, xs3, y, srcs4, dr4, "p4_")
    return P.realize()


def kernel_fused(L0_attn_x, L0_attn_W_qkv, L0_attn_W_o, L0_ffn_w_gate,
                 L0_ffn_w_up, L0_ffn_w_down, L1_attn_Wq, L1_attn_Wk,
                 L1_attn_Wv, L1_attn_Wo, L1_ffn_w1_w, L1_ffn_w1_s,
                 L1_ffn_w2_w, L1_ffn_w2_s, L1_ffn_h1_w, L1_ffn_h1_s,
                 L1_ffn_h2_w, L1_ffn_h2_s, L1_ffn_w3_w, L1_ffn_w3_s):
    n = 8
    consts = make_consts()
    x = np.ascontiguousarray(np.asarray(L0_attn_x, dtype=np.float32))

    def f32(a):
        return np.ascontiguousarray(np.asarray(a, dtype=np.float32))

    nc = build_fused()
    shared = dict(wqkv=f32(L0_attn_W_qkv), wo0=f32(L0_attn_W_o),
                  wg=f32(L0_ffn_w_gate), wu=f32(L0_ffn_w_up),
                  wd=f32(L0_ffn_w_down), wq=f32(L1_attn_Wq),
                  wk=f32(L1_attn_Wk), wv=f32(L1_attn_Wv), wo=f32(L1_attn_Wo),
                  f_w1=f32(L1_ffn_w1_w), f_w2=f32(L1_ffn_w2_w),
                  f_h1=f32(L1_ffn_h1_w), f_h2=f32(L1_ffn_h2_w),
                  f_w3=f32(L1_ffn_w3_w), v_s1=f32(L1_ffn_w1_s),
                  v_s2=f32(L1_ffn_w2_s), v_sh1=f32(L1_ffn_h1_s),
                  v_sh2=f32(L1_ffn_h2_s), v_s3=f32(L1_ffn_w3_s), **consts)
    in_maps = [dict(x=x[b], **shared) for b in range(n)]
    res = run_bass_kernel_spmd(nc, in_maps, core_ids=list(range(n)))
    return np.stack([np.asarray(res.results[b]["y"]) for b in range(n)], axis=0)


kernel = kernel_fused


# version 27
# speedup vs baseline: 1.0136x; 1.0136x over previous
import numpy as np
import ml_dtypes
from contextlib import ExitStack

import concourse.bass as bass
import concourse.mybir as mybir
from concourse.bass_utils import run_bass_kernel_spmd

F32 = mybir.dt.float32
BF16 = mybir.dt.bfloat16
I32 = mybir.dt.int32
AF = mybir.ActivationFunctionType
ALU = mybir.AluOpType
AX = mybir.AxisListType
ts = bass.ts

S = 4096
D = 1024
NH = 16
HD = 64
QS = 512
NQS = S // QS
DC = D // 128
NKT = S // 128

MAGIC = 0x5F3759DF

ENGINES = ("tensor", "vector", "scalar", "gpsimd", "sync")


class Sem:
    __slots__ = ("h", "n")

    def __init__(self, handle):
        self.h = handle
        self.n = 0


class Step:
    __slots__ = ("kind", "op", "args", "kwargs", "sem", "val")

    def __init__(self, kind, op=None, args=None, kwargs=None, sem=None, val=0):
        self.kind = kind
        self.op = op
        self.args = args
        self.kwargs = kwargs
        self.sem = sem
        self.val = val


class Eng:

    def __init__(self, prog, name):
        self.P = prog
        self.name = name

    def wait(self, sem, val=None):
        if val is None:
            val = sem.n
        if val > 0:
            self.P.steps[self.name].append(Step("wait", sem=sem, val=val))

    def op(self, opname, *args, inc=None, inc_by=1, **kwargs):
        st = Step("op", op=opname, args=args, kwargs=kwargs)
        if inc is not None:
            st.sem = inc
            st.val = inc_by
            inc.n += inc_by
        self.P.steps[self.name].append(st)
        return st

    def dma(self, out, in_, inc, **kwargs):
        return self.op("dma_start", out, in_, inc=inc, inc_by=16, **kwargs)

    def __getattr__(self, opname):
        def rec(*args, inc=None, inc_by=1, **kwargs):
            return self.op(opname, *args, inc=inc, inc_by=inc_by, **kwargs)
        return rec


class FencedEng:

    def __init__(self, eng, sem):
        self.e = eng
        self.s = sem

    def wait(self, sem, val=None):
        self.e.wait(sem, val)

    def __getattr__(self, opname):
        def rec(*args, inc=None, inc_by=1, **kwargs):
            st = self.e.op(opname, *args, inc=self.s, **kwargs)
            self.e.wait(self.s, self.s.n)
            if inc is not None:
                inc.n += inc_by
                st2 = Step("op", op="nop", args=(), kwargs={})
                st2.sem = inc
                st2.val = inc_by
                self.e.P.steps[self.e.name].append(st2)
            return st
        return rec


class Bank:
    __slots__ = ("ap", "free_sem", "free_val")

    def __init__(self, ap):
        self.ap = ap
        self.free_sem = None
        self.free_val = 0

    def acquire(self, eng):
        if self.free_sem is not None:
            eng.wait(self.free_sem, self.free_val)

    def release(self, sem, val=None):
        self.free_sem = sem
        self.free_val = sem.n if val is None else val


class Prog:
    def __init__(self, name="mk"):
        self.nc = bass.Bass("TRN2", target_bir_lowering=False, name=name)
        self.steps = {e: [] for e in ENGINES}
        self.ctx = ExitStack()
        self._sems = {}
        self.te = Eng(self, "tensor")
        self.ve = Eng(self, "vector")
        self.se = Eng(self, "scalar")
        self.ge = Eng(self, "gpsimd")
        self.sy = Eng(self, "sync")
        self.vf = FencedEng(self.ve, self.sem("s_vf"))
        self.sf = FencedEng(self.se, self.sem("s_af"))

    def sem(self, name):
        if name not in self._sems:
            h = self.ctx.enter_context(self.nc.semaphore(name))
            self._sems[name] = Sem(h)
        return self._sems[name]

    def sb(self, name, shape, dtype):
        return self.ctx.enter_context(self.nc.sbuf_tensor(name, shape, dtype))

    def ps(self, name, shape, dtype=F32):
        return self.ctx.enter_context(self.nc.psum_tensor(name, shape, dtype))

    def dram(self, name, shape, dtype, kind="Internal"):
        return self.nc.dram_tensor(name, shape, dtype, kind=kind)

    def barrier(self):
        b = self.sem("bar")
        for e in ENGINES:
            st = Step("op", op="nop", args=(), kwargs={})
            st.sem = b
            st.val = 1
            self.steps[e].append(st)
        b.n += 5
        for e in ENGINES:
            self.steps[e].append(Step("wait", sem=b, val=b.n))

    def realize(self):
        nc = self.nc
        with nc.Block() as block:
            for ename in ENGINES:
                steps = self.steps[ename]

                def make(fn_steps):
                    def run(eng):
                        for st in fn_steps:
                            if st.kind == "wait":
                                eng.wait_ge(st.sem.h, st.val)
                            else:
                                try:
                                    if st.op == "reciprocal":
                                        with nc.allow_low_precision(
                                                reason="bf16 softmax normalizer"):
                                            ins = getattr(eng, st.op)(
                                                *st.args, **st.kwargs)
                                    else:
                                        ins = getattr(eng, st.op)(*st.args,
                                                                  **st.kwargs)
                                except Exception:
                                    import sys
                                    print(f"FAILED {ename} op={st.op} "
                                          f"args={[type(a).__name__ for a in st.args]} "
                                          f"kwargs={list(st.kwargs)}",
                                          file=sys.stderr)
                                    raise
                                if st.sem is not None:
                                    ins.then_inc(st.sem.h, st.val)
                    return run
                getattr(block, ename)(make(steps))
        self.ctx.close()
        return nc



def emit_ln(P, out_ap, psum_ap, res_ap, y_sb, yjunk, st8, magic=None, inc=None,
            premul=None):
    v, se = P.vf, P.sf
    s_ae = P.sem("s_ae")
    s_ve = P.sem("s_ve")
    n = 1024
    if not isinstance(y_sb, bass.AP):
        y_sb = y_sb[:, :]
    if not isinstance(yjunk, bass.AP):
        yjunk = yjunk[:, :]
    if premul is not None:
        v.tensor_tensor(yjunk, psum_ap, premul, ALU.mult)
        v.scalar_tensor_tensor(y_sb, yjunk, 1.0, res_ap, ALU.mult, ALU.add,
                               accum_out=st8[:, 0:1])
    else:
        v.scalar_tensor_tensor(y_sb, psum_ap, 1.0, res_ap, ALU.mult, ALU.add,
                               accum_out=st8[:, 0:1])
    v.scalar_tensor_tensor(yjunk, y_sb, 1.0, y_sb, ALU.mult, ALU.mult,
                           accum_out=st8[:, 1:2])
    v.tensor_single_scalar(st8[:, 4:5], st8[:, 0:1], 1.0 / n, ALU.mult)
    v.tensor_tensor(st8[:, 2:3], st8[:, 4:5], st8[:, 4:5], ALU.mult)
    v.tensor_scalar(st8[:, 1:2], st8[:, 1:2], 1.0 / n, 1e-5, ALU.mult, ALU.add)
    v.tensor_tensor(st8[:, 2:3], st8[:, 1:2], st8[:, 2:3], ALU.subtract,
                    inc=s_ve)
    se.wait(P.sem("s_vf"))
    se.activation(st8[:, 5:6], st8[:, 2:3], AF.Ln)
    se.activation(st8[:, 3:4], st8[:, 5:6], AF.Exp, scale=-0.5, inc=s_ae)
    v.wait(s_ae)
    v.tensor_tensor(st8[:, 4:5], st8[:, 4:5], st8[:, 3:4], ALU.mult)
    v.tensor_single_scalar(st8[:, 4:5], st8[:, 4:5], -1.0, ALU.mult)
    v.tensor_scalar(out_ap, y_sb, st8[:, 3:4], st8[:, 4:5], ALU.mult, ALU.add,
                    inc=inc)


def make_consts():
    bf = ml_dtypes.bfloat16
    E = np.zeros((128, DC * 16), dtype=bf)
    for c in range(DC):
        for p in range(128):
            h = (128 * c + p) // HD
            E[p, c * 16 + h] = 1
    I16 = np.zeros((16, 16), dtype=bf)
    np.fill_diagonal(I16, 1)
    ident = np.eye(128, dtype=bf)
    ones1 = np.ones((128, 128), dtype=bf)
    ki = np.arange(128)[:, None]
    j0 = np.arange(1408)[None, :]
    mL0 = (np.abs(ki - (j0 - 640)) <= 256).astype(bf)
    j1 = np.arange(896)[None, :]
    mL1 = ((j1 - 384) >= ki).astype(bf)
    return dict(cE=E, cI16=I16, cones=ones1, cmL0=mL0, cmL1=mL1, cident=ident)


def emit_consts(P):
    nc = P.nc
    d = {}
    d["E_d"] = P.dram("cE", [128, DC * 16], BF16, kind="ExternalInput")
    d["I16_d"] = P.dram("cI16", [16, 16], BF16, kind="ExternalInput")
    d["ones_d"] = P.dram("cones", [128, 128], BF16, kind="ExternalInput")
    d["mL0_d"] = P.dram("cmL0", [128, 1408], BF16, kind="ExternalInput")
    d["mL1_d"] = P.dram("cmL1", [128, 896], BF16, kind="ExternalInput")
    d["ident_d"] = P.dram("cident", [128, 128], BF16, kind="ExternalInput")
    cs = {}
    cs["E"] = P.sb("c_E", [128, DC * 16], BF16)
    cs["I16"] = P.sb("c_I16", [16, 16], BF16)
    cs["ones1"] = P.sb("c_ones", [128, 128], BF16)
    cs["ident"] = P.sb("c_ident", [128, 128], BF16)
    cs["mL0_d"] = d["mL0_d"]
    cs["mL1_d"] = d["mL1_d"]

    d_c = P.sem("d_c")
    P.sy.dma(cs["E"][:, :], d["E_d"][:, :], inc=d_c)
    P.sy.dma(cs["I16"][:, :], d["I16_d"][:, :], inc=d_c)
    P.sy.dma(cs["ones1"][:, :], d["ones_d"][:, :], inc=d_c)
    P.sy.dma(cs["ident"][:, :], d["ident_d"][:, :], inc=d_c)
    ge_done = P.sem("s_misc")
    P.ge.nop(inc=ge_done)
    for e in (P.te, P.ve, P.se, P.ge):
        e.wait(d_c)
        e.wait(ge_done)
    return cs



def emit_attn(P, cs, xsrc, xdst, srcs, mode, pfx, debug=None, xgate=None):
    te, ve, se, ge, sy = P.te, P.ve, P.se, P.ge, P.sy
    nc = P.nc
    l0 = (mode == "l0")
    nslot = 3 if l0 else NQS

    phase_ctx = ExitStack()

    def sb(name, shape, dtype):
        return phase_ctx.enter_context(nc.sbuf_tensor(pfx + name, shape, dtype))

    def ps(name, shape, dtype=F32):
        return phase_ctx.enter_context(nc.psum_tensor(pfx + name, shape, dtype))

    qp = getattr(P, "_p1_qprep", None) if l0 else None
    if qp is not None:
        qp.alloc(phase_ctx)

    kT = sb("kT", [128, DC * nslot * QS], BF16)
    vv = sb("vv", [128, nslot * 4 * 1040], BF16)
    x_fm = [sb(f"xfm{i}", [128, DC * QS], BF16) for i in range(2)]
    sq = sb("sq", [128, QS], BF16)
    mk2 = sb("mk2", [16, 16], F32)

    pvp = [ps(f"pvp{i}", [128, 1024], F32).ap() for i in range(2)]
    pv_ps = [Bank(pvp[0][:, 0:512]), Bank(pvp[0][:, 512:1024]),
             Bank(pvp[1][:, 0:512]), Bank(pvp[1][:, 512:1024])]
    s01 = ps("s01", [128, 1024], F32).ap()
    s_ps = [Bank(s01[:, 0:512]), Bank(s01[:, 512:1024])]
    o_ps = Bank(ps("o", [128, 1024], F32).ap())
    ovA = Bank(o_ps.ap[:, 0:512])
    ovB = Bank(o_ps.ap[:, 512:1024])

    d_w = P.sem(pfx + "d_w")
    d_xf = [P.sem(pfx + f"d_xf{i}") for i in range(2)]
    d_xt = [P.sem(pfx + f"d_xt{i}") for i in range(2)]
    d_out = P.sem("d_out")
    s_pe = P.sem("s_pe")
    s_ae = P.sem("s_ae")
    s_ve = P.sem("s_ve")
    s_ge = P.sem("s_ge")
    s_gp = P.sem("s_gp")

    xf_loads = [0, 0]

    def load_xfm(s):
        b = s % 2
        if xgate is not None:
            xgate(s)
        for c in range(DC):
            sy.dma(x_fm[b][:, c * QS:(c + 1) * QS],
                   xsrc[s * QS:(s + 1) * QS, c * 128:(c + 1) * 128],
                   inc=d_xf[b], transpose=True)
        xf_loads[b] = d_xf[b].n

    for t in range(nslot):
        for j in range(4):
            base = (t * 4 + j) * 1040
            ge.memset(vv[:, base:base + 1040].rearrange(
                "p (h w) -> p h w", h=NH)[:, :, 64:65], 1.0)
    ge.memset(mk2[0:16, 0:1], 0.0, inc=s_ge)
    te.wait(s_ge)
    ve.wait(s_ge)

    def kv_slot(sl, wk_ap, wv_ap, xb, xb_sem, xb_val):
        t = sl % nslot
        te.wait(xb_sem, xb_val)
        for oc in range(DC):
            bk = s_ps[oc % 2]
            bk.acquire(te)
            for ic in range(DC):
                te.matmul(bk.ap, wk_ap(ic, oc),
                          xb[:, ic * QS:(ic + 1) * QS],
                          start=(ic == 0), stop=(ic == DC - 1),
                          inc=s_pe if ic == DC - 1 else None)
            se.wait(s_pe)
            se.activation(kT[:, (oc * nslot + t) * QS:(oc * nslot + t + 1) * QS],
                          bk.ap, AF.Copy, inc=s_ae)
            bk.release(s_ae)
        bk = s_ps[0]
        bk.acquire(te)
        for oc in range(DC):
            se.wait(s_ae)
            se.activation(sq[:, :],
                          kT[:, (oc * nslot + t) * QS:(oc * nslot + t + 1) * QS],
                          AF.Square, inc=s_ae)
            te.wait(s_ae)
            te.matmul(bk.ap[0:16, :], cs["E"][:, oc * 16:(oc + 1) * 16],
                      sq[:, :], start=(oc == 0), stop=(oc == DC - 1),
                      inc=s_pe if oc == DC - 1 else None)
        ve.wait(s_pe)
        P.vf.tensor_reduce(mk2[0:16, 1:2], bk.ap[0:16, :], AX.X, ALU.max)
        P.vf.tensor_tensor(mk2[0:16, 0:1], mk2[0:16, 0:1], mk2[0:16, 1:2],
                           ALU.max, inc=s_ve)
        bk.release(s_ve)
        for j in range(4):
            for half in range(2):
                bk = s_ps[half]
                bk.acquire(te)
                for ic in range(DC):
                    te.matmul(bk.ap,
                              xb[:, ic * QS + j * 128:ic * QS + (j + 1) * 128],
                              wv_ap(ic, half),
                              start=(ic == 0), stop=(ic == DC - 1),
                              inc=s_pe if ic == DC - 1 else None)
            base = (t * 4 + j) * 1040
            ve.wait(s_pe)
            for half in range(2):
                dst = vv[:, base:base + 1040].rearrange(
                    "p (h w) -> p h w", h=NH)[:, half * 8:(half + 1) * 8, 0:64]
                ve.tensor_copy(dst, s_ps[half].ap.rearrange(
                    "p (h w) -> p h w", w=64), inc=s_ve)
                s_ps[half].release(s_ve)

    if l0:
        wqkvT = sb("wqkvT", [128, DC * 3072], BF16)
        for c in range(DC):
            sy.dma(wqkvT[:, c * 3072:(c + 1) * 3072],
                   srcs["qkv"][:, c * 128:(c + 1) * 128], inc=d_w, transpose=True)
        woT = sb("woT", [128, DC * 1024], BF16)
        for c in range(DC):
            sy.dma(woT[:, c * 1024:(c + 1) * 1024],
                   srcs["o"][:, c * 128:(c + 1) * 128], inc=d_w, transpose=True)

        def wk_ap(ic, oc):
            return wqkvT[:, ic * 3072 + 1024 + oc * 128:ic * 3072 + 1024 + (oc + 1) * 128]

        def wv_ap(ic, half):
            return wqkvT[:, ic * 3072 + 2048 + half * 512:ic * 3072 + 2048 + (half + 1) * 512]

        def wq_ap(ic, oc):
            return wqkvT[:, ic * 3072 + oc * 128:ic * 3072 + (oc + 1) * 128]

        def wo_ap(ic, half):
            return woT[:, ic * 1024 + half * 512:ic * 1024 + (half + 1) * 512]

        te.wait(d_w)
        se.wait(d_w)
        load_xfm(0)
        load_xfm(1)
        kv_slot(0, wk_ap, wv_ap, x_fm[0], d_xf[0], xf_loads[0])
        if NQS > 1:
            kv_slot(1, wk_ap, wv_ap, x_fm[1], d_xf[1], xf_loads[1])
    else:
        pre_ctx = ExitStack()
        wkT = pre_ctx.enter_context(nc.sbuf_tensor(pfx + "wkT", [128, DC * 1024], BF16))
        wvT = pre_ctx.enter_context(nc.sbuf_tensor(pfx + "wvT", [128, DC * 1024], BF16))
        for c in range(DC):
            sy.dma(wkT[:, c * 1024:(c + 1) * 1024],
                   srcs["k"][:, c * 128:(c + 1) * 128], inc=d_w, transpose=True)
            sy.dma(wvT[:, c * 1024:(c + 1) * 1024],
                   srcs["v"][:, c * 128:(c + 1) * 128], inc=d_w, transpose=True)

        def wk_ap(ic, oc):
            return wkT[:, ic * 1024 + oc * 128:ic * 1024 + (oc + 1) * 128]

        def wv_ap(ic, half):
            return wvT[:, ic * 1024 + half * 512:ic * 1024 + (half + 1) * 512]

        te.wait(d_w)
        se.wait(d_w)
        load_xfm(0)
        for sl in range(NQS):
            if sl + 1 < NQS:
                if sl > 0:
                    sy.wait(s_pe)
                load_xfm(sl + 1)
            kv_slot(sl, wk_ap, wv_ap, x_fm[sl % 2], d_xf[sl % 2],
                    xf_loads[sl % 2])
        P.barrier()
        pre_ctx.close()
        wbuf = sb("wbuf", [128, DC * 1024], BF16)
        wq_chunk = [0] * DC
        wo_loaded = [0]

        def wq_ap(ic, oc):
            return wbuf[:, ic * 1024 + oc * 128:ic * 1024 + (oc + 1) * 128]

        def wo_ap(ic, half):
            return wbuf[:, ic * 1024 + half * 512:ic * 1024 + (half + 1) * 512]

    q_fm = sb("qfm", [128, DC * QS], BF16)
    p_buf = sb("pbuf", [128, 12 * QS], BF16)
    ao_fm = sb("ao", [128, DC * QS], BF16)
    x_tm = [sb(f"xtm{i}", [128, 1024], BF16) for i in range(2)]
    recipb = sb("recipb", [128, 3 * QS], BF16)
    small = sb("small", [128, 32], F32)
    mrow = sb("mrow", [16, 16], BF16)
    y_sb = sb("y", [128, 1024], F32)
    yjunk = sb("yjunk", [128, 1024], BF16)
    st8 = sb("st8", [128, 8], F32)
    x1t = [sb("x1t0", [128, 1024], BF16)] * 2
    mwide = 1408 if l0 else 896
    mbase = 896 if l0 else 384
    masksb = sb("masks", [128, mwide], BF16)
    d_msk = P.sem(pfx + "d_msk")
    sy.dma(masksb[:, :], (cs["mL0_d"] if l0 else cs["mL1_d"])[:, :], inc=d_msk)
    ve.wait(d_msk)

    exp_done = {}
    pv_done = {}
    consumed_sem = {}
    consumed_val = {}
    ready_sem = {}
    ready_val = {}
    gu = [0]
    for s in range(NQS):
        if not l0:
            sy.wait(s_pe)
            for c in range(DC):
                sy.dma(wbuf[:, c * 1024:(c + 1) * 1024],
                       srcs["q"][:, c * 128:(c + 1) * 128], inc=d_w,
                       transpose=True)
                wq_chunk[c] = d_w.n
            if s == 0:
                load_xfm(0)
        else:
            if s + 1 < NQS and s > 0:
                kv_slot(s + 1, wk_ap, wv_ap, x_fm[(s + 1) % 2],
                        d_xf[(s + 1) % 2], xf_loads[(s + 1) % 2])
        te.wait(d_xf[s % 2], xf_loads[s % 2])
        ovA.free_sem, ovA.free_val = o_ps.free_sem, o_ps.free_val
        ovB.free_sem, ovB.free_val = o_ps.free_sem, o_ps.free_val
        all8 = [s_ps[0], s_ps[1], ovA, ovB] + pv_ps
        for oc in range(DC):
            all8[oc].acquire(te)
        for ic in range(DC):
            if not l0:
                te.wait(d_w, wq_chunk[ic])
            for oc in range(DC):
                te.matmul(all8[oc].ap, wq_ap(ic, oc),
                          x_fm[s % 2][:, ic * QS:(ic + 1) * QS],
                          start=(ic == 0), stop=(ic == DC - 1),
                          inc=s_pe if ic == DC - 1 and oc == DC - 1 else None)
        se.wait(s_pe)
        for oc in range(DC):
            se.activation(q_fm[:, oc * QS:(oc + 1) * QS], all8[oc].ap,
                          AF.Copy, inc=s_ae)
            all8[oc].release(s_ae)
        if not l0:
            sy.wait(s_pe)
            for c in range(DC):
                sy.dma(wbuf[:, c * 1024:(c + 1) * 1024],
                       srcs["o"][:, c * 128:(c + 1) * 128], inc=d_w,
                       transpose=True)
            wo_loaded[0] = d_w.n
        nxt = s + 2 if l0 else s + 1
        if nxt < NQS:
            sy.wait(s_pe)
            load_xfm(nxt)

        bk = s_ps[0]
        bk.acquire(te)
        for oc in range(DC):
            se.wait(s_ae)
            se.activation(sq[:, :], q_fm[:, oc * QS:(oc + 1) * QS], AF.Square,
                          inc=s_ae)
            te.wait(s_ae)
            te.matmul(bk.ap[0:16, :], cs["E"][:, oc * 16:(oc + 1) * 16],
                      sq[:, :], start=(oc == 0), stop=(oc == DC - 1),
                      inc=s_pe if oc == DC - 1 else None)
        ve.wait(s_pe)
        vf, sf = P.vf, P.sf
        vf.tensor_reduce(mk2[0:16, 2:3], bk.ap[0:16, :], AX.X, ALU.max)
        vf.tensor_tensor(mk2[0:16, 3:4], mk2[0:16, 2:3], mk2[0:16, 0:1],
                         ALU.mult)
        vf.tensor_scalar(mk2[0:16, 3:4], mk2[0:16, 3:4], 1.0 / (8.0 * 8.0),
                         1e-30, ALU.mult, ALU.max)
        sf.wait(P.sem("s_vf"))
        sf.activation(mk2[0:16, 4:5], mk2[0:16, 3:4], AF.Ln)
        sf.activation(mk2[0:16, 5:6], mk2[0:16, 4:5], AF.Exp, scale=0.5,
                      inc=s_ae)
        vf.wait(s_ae)
        vf.tensor_single_scalar(mk2[0:16, 6:7], mk2[0:16, 5:6], -1.0, ALU.mult)
        vf.tensor_copy(mrow[0:16, 0:1], mk2[0:16, 6:7], inc=s_ve)
        te.wait(s_ve)
        te.matmul(bk.ap[0:1, 0:16], mrow[0:16, 0:1], cs["I16"][0:16, 0:16],
                  start=True, stop=True, inc=s_pe)
        se.wait(s_pe)
        se.activation(mrow[0:1, 0:16], bk.ap[0:1, 0:16], AF.Copy, inc=s_ae)
        te.wait(s_ae)
        te.matmul(bk.ap[:, 0:16], cs["ones1"][0:1, 0:128], mrow[0:1, 0:16],
                  start=True, stop=True, inc=s_pe)
        se.wait(s_pe)
        se.activation(small[:, 0:16], bk.ap[:, 0:16], AF.Copy, inc=s_ae)
        bias_ready = s_ae.n
        bk.release(s_ae)
        ve.wait(s_ae, bias_ready)
        P.vf.tensor_tensor(small[:, 16:24],
                           small[:, 0:16].rearrange("p (a b) -> p a b", b=2)[:, :, 0],
                           small[:, 0:16].rearrange("p (a b) -> p a b", b=2)[:, :, 1],
                           ALU.min, inc=s_ve)
        te.wait(s_ve)
        se.wait(s_ve)

        if l0:
            kts = list(range(max(0, 4 * s - 2), min(NKT, 4 * s + 6)))
        else:
            kts = list(range(0, 4 * (s + 1)))
        pend_epi = [None]
        slot_exp = {}

        def emit_epilogue(w2):
            hA, hB = 2 * w2, 2 * w2 + 1
            bks = [pv_ps[2 * (w2 % 2)], pv_ps[2 * (w2 % 2) + 1]]
            sl = [slots[2 * (w2 % 2)], slots[2 * (w2 % 2) + 1]]
            ve.wait(s_pe, pv_last[w2])
            for z in (0, 1):
                ve.reciprocal(sl[z][64:65, :], bks[z].ap[64:65, :],
                              inc=s_ve if z == 1 else None)
            te.wait(s_ve)
            for z in (0, 1):
                te.matmul(bks[z].ap[64:128, :], cs["ones1"][64:65, 0:64],
                          sl[z][64:65, :], start=True, stop=True,
                          inc=s_pe if z == 1 else None)
            se.wait(s_pe)
            for z in (0, 1):
                se.activation(sl[z][0:64, :], bks[z].ap[64:128, :], AF.Copy,
                              inc=s_ae if z == 1 else None)
            ve.wait(s_ae)
            for z in (0, 1):
                h = (hA, hB)[z]
                c = h // 2
                ro = (h % 2) * 64
                ve.tensor_tensor(ao_fm[ro:ro + 64, c * QS:(c + 1) * QS],
                                 bks[z].ap[0:64, :], sl[z][0:64, :], ALU.mult,
                                 inc=s_ve)
                bks[z].release(s_ve)

        slots = [recipb[:, 0:QS], recipb[:, QS:2 * QS],
                 recipb[:, 2 * QS:3 * QS], sq[:, 0:QS]]
        pv_last = {}
        for w in range(8):
            bkA = pv_ps[2 * (w % 2)]
            bkB = pv_ps[2 * (w % 2) + 1]
            bkA.acquire(te)
            bkB.acquire(te)
            te.wait(s_ae, slot_exp.get(2 + (w % 2), 0))
            sl3 = [(0, s01, (s_ps[0], s_ps[1])),
                   (1, o_ps.ap, (ovA, ovB)),
                   (2 + ((w + 1) % 2), pvp[(w + 1) % 2],
                    (pv_ps[2 * ((w + 1) % 2)], pv_ps[2 * ((w + 1) % 2) + 1]))]
            fresh = {k: True for k in range(3)}
            LAG = 2
            pend = []

            def emit_pv(info):
                i2, kt2, pb2 = info
                t2 = (kt2 // 4) % nslot
                te.wait(ready_sem[i2], ready_val[i2])
                vbase = (t2 * 4 + (kt2 % 4)) * 1040
                for z in (0, 1):
                    h2 = 2 * w + z
                    vb = vv[:, vbase:vbase + 1040].rearrange(
                        "p (h x) -> p h x", h=NH)[:, h2, :]
                    te.matmul((bkA, bkB)[z].ap[0:65, :], vb,
                              pb2[:, z * QS:(z + 1) * QS],
                              start=(kt2 == kts[0]), stop=(kt2 == kts[-1]),
                              inc=s_pe)
                pv_done[i2] = s_pe.n
                if consumed_sem[i2] is s_pe:
                    consumed_val[i2] = s_pe.n

            for iw, kt in enumerate(kts):
                i = gu[0]
                gu[0] += 1
                slot_id, pbk, halves = sl3[iw % 3]
                c = w
                t = (kt // 4) % nslot
                within = (kt % 4) * 128
                if fresh[iw % 3]:
                    fresh[iw % 3] = False
                    halves[0].acquire(te)
                    halves[1].acquire(te)
                    te.wait(s_ae, slot_exp.get(slot_id, 0))
                else:
                    te.wait(s_ae, exp_done[i - 3])
                for z in (0, 1):
                    ro = z * 64
                    lhsT = kT[ro:ro + 64,
                              (c * nslot + t) * QS + within:
                              (c * nslot + t) * QS + within + 128]
                    rhs = q_fm[ro:ro + 64, c * QS:(c + 1) * QS]
                    te.matmul(pbk[:, z * QS:(z + 1) * QS], lhsT, rhs,
                              start=True, stop=True,
                              inc=s_pe if z == 1 else None)
                se.wait(s_pe)
                pslot = p_buf[:, (i % 3) * 1024:(i % 3) * 1024 + 1024]
                if i >= 3:
                    se.wait(consumed_sem[i - 3], consumed_val[i - 3])
                se.activation(pslot, pbk[:, 0:1024], AF.Exp,
                              bias=small[:, 16 + w:17 + w],
                              scale=0.125, inc=s_ae)
                exp_done[i] = s_ae.n
                slot_exp[slot_id] = s_ae.n
                delta = kt * 128 - s * QS
                need_mask = l0 or (delta >= 0)
                if need_mask:
                    mi = (delta + 256) // 128 if l0 else delta // 128
                    mslot = p_buf[:, 3072 + (i % 3) * 1024:
                                  3072 + (i % 3) * 1024 + 1024]
                    ge.wait(s_ae, exp_done[i])
                    if i >= 3:
                        ge.wait(s_pe, pv_done[i - 3])
                    mlo = mbase - 128 * mi
                    ge.tensor_tensor(
                        mslot.rearrange("p (z q) -> p z q", z=2),
                        pslot.rearrange("p (z q) -> p z q", z=2),
                        masksb[:, mlo:mlo + QS].unsqueeze(1)
                        .broadcast_to([128, 2, QS]),
                        ALU.mult, inc=s_gp)
                    ready_sem[i] = s_gp
                    ready_val[i] = s_gp.n
                    consumed_sem[i] = s_gp
                    consumed_val[i] = s_gp.n
                    pslot = mslot
                else:
                    ready_sem[i] = s_ae
                    ready_val[i] = exp_done[i]
                    consumed_sem[i] = s_pe
                pend.append((i, kt, pslot))
                if len(pend) > LAG:
                    emit_pv(pend.pop(0))
                if iw == 1 and pend_epi[0] is not None:
                    pend_epi[0]()
                    pend_epi[0] = None
            while pend:
                emit_pv(pend.pop(0))
            pv_last[w] = s_pe.n
            if pend_epi[0] is not None:
                pend_epi[0]()
            pend_epi[0] = (lambda w2=w: emit_epilogue(w2))
        pend_epi[0]()
        pend_epi[0] = None
        for bk4 in (s_ps[0], s_ps[1], ovA, ovB):
            bk4.release(s_ae)
        o_ps.release(s_ae)
        pv_ps[0].release(s_ae)
        pv_ps[1].release(s_ae)

        te.wait(s_ve)
        if not l0:
            te.wait(d_w, wo_loaded[0])
        for j in range(4):
            bsel = (s * 4 + j) % 2
            sy.wait(s_ve)
            src = xsrc[(s * QS + j * 128):(s * QS + (j + 1) * 128), :]
            sy.dma(x_tm[bsel][:, :], src, inc=d_xt[bsel])
            o_ps.acquire(te)
            for half in range(2):
                for ic in range(DC):
                    te.matmul(o_ps.ap[:, half * 512:(half + 1) * 512],
                              ao_fm[:, ic * QS + j * 128:ic * QS + (j + 1) * 128],
                              wo_ap(ic, half),
                              start=(ic == 0), stop=(ic == DC - 1),
                              inc=s_pe if ic == DC - 1 else None)
            ve.wait(s_pe)
            ve.wait(d_xt[bsel])
            ve.wait(d_out)
            emit_ln(P, x1t[bsel][:, :], o_ps.ap[:, :], x_tm[bsel][:, :],
                    y_sb, yjunk, st8, inc=s_ve)
            o_ps.release(s_ve)
            sy.wait(s_ve)
            sy.dma(xdst[(s * QS + j * 128):(s * QS + (j + 1) * 128), :],
                   x1t[bsel][:, :], inc=d_out)

    if debug is not None:
        d_dbg = P.sem(pfx + "d_dbg")
        sy.wait(s_ve)
        sy.wait(s_ae)
        if "q_fm" in debug:
            sy.dma(debug["q_fm"][:, :], q_fm[:, :], inc=d_dbg)
        if "ao_fm" in debug:
            sy.dma(debug["ao_fm"][:, :], ao_fm[:, :], inc=d_dbg)
        if "small" in debug:
            sy.dma(debug["small"][:, :], small[:, :], inc=d_dbg)
        if "mk2" in debug:
            sy.dma(debug["mk2"][:, :], mk2[:, :], inc=d_dbg)
        if "kT" in debug:
            sy.dma(debug["kT"][:, :], kT[:, :], inc=d_dbg)
        if "vv" in debug:
            sy.dma(debug["vv"][:, :], vv[:, :], inc=d_dbg)
        if "pbuf" in debug:
            sy.dma(debug["pbuf"][:, :], p_buf[:, :], inc=d_dbg)
        sy.wait(d_dbg)
    sy.wait(d_out)
    P.barrier()
    phase_ctx.close()


def _apslice(ap, lo, hi):
    if not isinstance(ap, bass.AP):
        return ap[:, lo:hi]
    t = ap
    return bass.AP(t.tensor, t.offset + lo, [list(t.ap[0]), [1, hi - lo]])


def emit_quant(P, out_ap, in_ap, scr, n, pool_cols=0):
    vf, sf = P.ve, P.sf
    s_af = P.sem("s_af")
    s_vf = P.sem("s_vf")
    s_gp = P.sem("s_gp")
    nb = n // 16
    A = scr["A"][:, 0:n]
    Bt = scr["B"][:, 0:n]
    C = scr["C"][:, 0:n]
    mx = scr["mx"][:, 0:nb]
    sc = scr["sc"][:, 0:nb]
    rs = scr["rs"][:, 0:nb]
    in3 = in_ap.rearrange("p (b k) -> p b k", k=16)
    sf.wait(s_vf)
    sf.wait(s_gp)
    sf.activation(A, in_ap, AF.Abs)
    sf.activation(Bt, in_ap, AF.Sign)
    P.vf.tensor_reduce(mx, in3, AX.X, ALU.max, apply_absolute_value=True)
    P.vf.tensor_scalar(sc, mx, 1.0 / 6.0, 1e-12, ALU.mult, ALU.max)
    P.vf.reciprocal(rs, sc)
    vf.wait(s_af)
    nd = n - pool_cols
    assert nd % 16 == 0

    def chain(e, lo, hi):
        if hi <= lo:
            return
        b0, b1 = lo // 16, hi // 16
        Av = A[:, lo:hi].rearrange("p (b k) -> p b k", k=16)
        Cv = C[:, lo:hi]
        rsv = rs[:, b0:b1].unsqueeze(2).broadcast_to([128, b1 - b0, 16])
        scv = sc[:, b0:b1].unsqueeze(2).broadcast_to([128, b1 - b0, 16])
        e.tensor_tensor(Av, Av, rsv, ALU.mult)
        e.tensor_single_scalar(Cv, A[:, lo:hi], 0.25, ALU.is_gt)
        for m in (0.75, 1.25, 1.75):
            e.scalar_tensor_tensor(Cv, A[:, lo:hi], m, Cv, ALU.is_gt, ALU.add)
        e.tensor_single_scalar(Cv, Cv, 0.5, ALU.mult)
        for m in (2.5, 3.5, 5.0, 5.0):
            e.scalar_tensor_tensor(Cv, A[:, lo:hi], m, Cv, ALU.is_gt, ALU.add)
        e.tensor_tensor(Av, C[:, lo:hi].rearrange("p (b k) -> p b k", k=16),
                        scv, ALU.mult)
    if pool_cols > 0:
        P.ge.wait(s_af)
        P.ge.wait(s_vf)
        P.ge.wait(P.sem("s_pe"), P._quant_pe_guard) if False else None
        chain(P.ge, nd, n)
        P.ge.op("tensor_tensor", out_ap.tensor[:, 0:1] if False else None)             if False else None
        P.ge.tensor_tensor(
            out_ap if not isinstance(out_ap, bass.AP) else out_ap, None, None,
            None) if False else None
    chain(vf, 0, nd)
    if pool_cols > 0:
        P.vf.tensor_tensor(_apslice(out_ap, 0, nd), A[:, 0:nd], Bt[:, 0:nd],
                           ALU.mult)
        P.ge.tensor_tensor(_apslice(out_ap, nd, n), A[:, nd:n], Bt[:, nd:n],
                           ALU.mult, inc=s_gp)
    elif getattr(P, "_pool_finals", False):
        P.ve.op("tensor_copy", mx[:, 0:16], C[:, 0:16], inc=P.sem("s_ve"))
        P.ge.wait(P.sem("s_ve"))
        P.ge.wait(s_af)
        P.ge.tensor_tensor(A.rearrange("p (b k) -> p b k", k=16),
                           C.rearrange("p (b k) -> p b k", k=16),
                           sc.unsqueeze(2).broadcast_to([128, nb, 16]),
                           ALU.mult)
        P.ge.tensor_tensor(out_ap, A, Bt, ALU.mult, inc=s_gp)
    else:
        P.vf.tensor_tensor(out_ap, A, Bt, ALU.mult)


class QPrep:

    def __init__(self, P, jobs, d_prep):
        self.P = P
        self.d_prep = d_prep
        self.units = []
        for (w_f32, wq_bf, O, I) in jobs:
            npc = (I + 703) // 704
            base = I // npc
            assert base % 16 == 0 and base * npc == I
            for r in range(O // 128):
                for q in range(npc):
                    self.units.append((w_f32, wq_bf, r, q * base,
                                       (q + 1) * base))
        self.idx = 0
        self.qvf = [0, 0]
        self.qaf = [0, 0]
        self.loads = [0, 0]
        self.sdone = 0
        self.ld_idx = 0

    def alloc(self, ctx):
        nc = self.P.nc
        sbt = lambda n, s, d: ctx.enter_context(nc.sbuf_tensor(n, s, d))
        self.W = [sbt(f"qpW{i}", [128, 704], F32) for i in range(2)]
        self.WQ = sbt("qpWQ", [128, 704], BF16)
        self.scr = {
            "A": sbt("qpA", [128, 704], BF16),
            "B": sbt("qpB", [128, 704], BF16),
            "C": sbt("qpC", [128, 704], BF16),
            "mx": sbt("qpmx", [128, 44], F32),
            "sc": sbt("qpsc", [128, 44], F32),
            "rs": sbt("qprs", [128, 44], F32),
        }
        self.d_q = [self.P.sem("qp_dA"), self.P.sem("qp_dB")]

    def _load(self):
        if self.ld_idx >= len(self.units):
            return
        P = self.P
        w_f32, _, r, c0, c1 = self.units[self.ld_idx]
        b = self.ld_idx % 2
        P.sy.wait(P.sem("s_vf"), self.qvf[b])
        P.sy.wait(P.sem("s_af"), self.qaf[b])
        P.sy.dma(self.W[b][:, 0:c1 - c0],
                 w_f32[r * 128:(r + 1) * 128, c0:c1], inc=self.d_q[b])
        self.loads[b] = self.d_q[b].n
        self.ld_idx += 1

    def emit(self, k):
        P = self.P
        end = min(len(self.units), self.idx + k)
        while self.ld_idx < min(len(self.units), self.idx + 2):
            self._load()
        while self.idx < end:
            w_f32, wq_bf, r, c0, c1 = self.units[self.idx]
            b = self.idx % 2
            P.ve.wait(self.d_q[b], self.loads[b])
            P.se.wait(self.d_q[b], self.loads[b])
            P.ve.wait(self.d_prep, self.sdone)
            emit_quant(P, self.WQ[:, 0:c1 - c0], self.W[b][:, 0:c1 - c0],
                       self.scr, c1 - c0, pool_cols=0)
            self.qvf[b] = P.sem("s_vf").n
            self.qaf[b] = P.sem("s_af").n
            P.sy.wait(P.sem("s_vf"), self.qvf[b])
            P.sy.wait(P.sem("s_gp"))
            P.sy.dma(wq_bf[r * 128:(r + 1) * 128, c0:c1], self.WQ[:, 0:c1 - c0],
                     inc=self.d_prep)
            self.sdone = self.d_prep.n
            self.idx += 1
            self._load()

    def finish(self):
        self.emit(len(self.units))
        self.P.sy.wait(self.d_prep, self.sdone)


def emit_qprep(P, w_f32, wq_bf, O, I, scr, d_prep):
    sy = P.sy
    s_vf = P.sem("s_vf")
    s_af = P.sem("s_af")
    d_q = [P.sem("d_qpA"), P.sem("d_qpB")]
    Wt = [scr["W"], scr["W2"]]
    wq = scr["WQ"]
    st = P._qprep_state = getattr(P, "_qprep_state",
                                  {"qvf": [0, 0], "qaf": [0, 0],
                                   "loads": [0, 0], "sdone": 0})
    nR = O // 128

    def load(r):
        b = r % 2
        sy.wait(s_vf, st["qvf"][b])
        sy.wait(s_af, st["qaf"][b])
        sy.dma(Wt[b][:, 0:I], w_f32[r * 128:(r + 1) * 128, :], inc=d_q[b])
        st["loads"][b] = d_q[b].n

    load(0)
    for r in range(nR):
        b = r % 2
        if r + 1 < nR:
            load(r + 1)
        P.ve.wait(d_q[b], st["loads"][b])
        P.se.wait(d_q[b], st["loads"][b])
        P.ve.wait(d_prep, st["sdone"])
        emit_quant(P, wq[:, 0:I], Wt[b][:, 0:I], scr, I)
        st["qvf"][b] = P.sem("s_vf").n
        st["qaf"][b] = P.sem("s_af").n
        sy.wait(s_vf)
        sy.dma(wq_bf[r * 128:(r + 1) * 128, :], wq[:, 0:I], inc=d_prep)
        st["sdone"] = d_prep.n



def emit_ffn0(P, cs, xsrc, xdst, srcs, dr, pfx):
    te, ve, se, ge, sy = P.te, P.ve, P.se, P.ge, P.sy
    nc = P.nc
    II = 2816
    NIC = II // 128
    phase_ctx = ExitStack()

    def sb(name, shape, dtype):
        return phase_ctx.enter_context(nc.sbuf_tensor(pfx + name, shape, dtype))

    def ps(name, shape, dtype=F32):
        return phase_ctx.enter_context(nc.psum_tensor(pfx + name, shape, dtype))

    d_w = P.sem(pfx + "d_w")
    d_x = [P.sem(pfx + f"d_x{i}") for i in range(1)]
    d_out = P.sem("d_out")
    s_pe = P.sem("s_pe")
    s_ae = P.sem("s_ae")
    s_ve = P.sem("s_ve")
    s_vf = P.sem("s_vf")
    s_af = P.sem("s_af")

    wgT = sb("wgT", [128, DC * II], BF16)
    wuT = sb("wuT", [128, DC * II], BF16)
    wdT = sb("wdT", [128, NIC * 1024], BF16)
    x_tm2 = [sb(f"xtm{i}", [128, 4 * 1024], BF16) for i in range(2)]
    xq_fm = sb("xqfm", [128, DC * QS], BF16)
    g_sb = sb("g", [128, II], BF16)
    h_sb = sb("h", [128, II], BF16)
    hq_sb = sb("hq", [128, II], BF16)
    hqT = sb("hqT", [128, II], BF16)
    scr = {
        "A": sb("scrA", [128, II], BF16),
        "B": sb("scrB", [128, II], BF16),
        "C": sb("scrC", [128, II], BF16),
        "mx": sb("scrmx", [128, 176], F32),
        "sc": sb("scrsc", [128, 176], F32),
        "rs": sb("scrrs", [128, 176], F32),
    }
    y_sb = sb("y", [128, 1024], F32)
    st8 = sb("st8", [128, 8], F32)
    x2t = sb("x2t", [128, 1024], BF16)
    yjunk = x2t
    pA = Bank(ps("pA", [128, 512], F32).ap())
    pB = Bank(ps("pB", [128, 512], F32).ap())
    pT = [Bank(ps(f"pT{i}", [128, 1024], BF16).ap()) for i in range(2)]
    o_ps2 = [Bank(ps(f"o{i}", [128, 1024], F32).ap()) for i in range(2)]

    prep4 = getattr(P, "_prep4", None)
    if prep4 is not None:
        sy.wait(P.sem("d_cast"), P._prep4_c4)
        prep4.alloc(phase_ctx)
    P._pool_finals = True

    for c in range(DC):
        sy.dma(wgT[:, c * II:(c + 1) * II], srcs["gate"][:, c * 128:(c + 1) * 128],
               inc=d_w, transpose=True)
        sy.dma(wuT[:, c * II:(c + 1) * II], srcs["up"][:, c * 128:(c + 1) * 128],
               inc=d_w, transpose=True)
    for c in range(NIC):
        sy.dma(wdT[:, c * 1024:(c + 1) * 1024],
               srcs["down"][:, c * 128:(c + 1) * 128], inc=d_w, transpose=True)
    te.wait(d_w)

    x_load = [0]
    xt_done = [0]
    ht_done = [0]
    down_done = {}
    ident = cs["ident"]

    def emit_deferred_ln(gj):
        g2, j2 = divmod(gj, 4)
        o_ps = o_ps2[gj % 2]
        ve.wait(s_pe, down_done[gj])
        ve.wait(d_out)
        emit_ln(P, x2t[:, :], o_ps.ap[:, :],
                x_tm2[g2 % 2][:, j2 * 1024:(j2 + 1) * 1024],
                y_sb, yjunk, st8, inc=s_ve)
        o_ps.release(s_ve)
        sy.wait(s_ve)
        sy.dma(xdst[g2 * QS + j2 * 128:g2 * QS + (j2 + 1) * 128, :],
               x2t[:, :], inc=d_out)

    for g in range(S // QS):
        sy.wait(s_vf)
        sy.wait(s_af)
        sy.wait(s_ve)
        src2 = xsrc[g * QS:(g + 1) * QS, :].rearrange("(t p) d -> p t d", p=128)
        sy.dma(x_tm2[g % 2][:, :].rearrange("p (t d) -> p t d", t=4), src2,
               inc=d_x[0])
        x_load[0] = d_x[0].n
        P.ve.wait(d_x[0], x_load[0])
        P.se.wait(d_x[0], x_load[0])
        if prep4 is not None:
            prep4.emit(13)
        for j in range(4):
            P.ve.wait(s_pe, ht_done[0])
            emit_quant(P, hq_sb[:, 0:1024],
                       x_tm2[g % 2][:, j * 1024:(j + 1) * 1024], scr, 1024,
                       pool_cols=0)
            bk = pT[0]
            bk.acquire(te)
            te.wait(P.sem("s_vf"))
            te.wait(P.sem("s_gp"))
            for c in range(DC):
                ins = te.op("transpose", bk.ap[:, c * 128:(c + 1) * 128],
                            hq_sb[:, c * 128:(c + 1) * 128], ident[:, :],
                            inc=s_pe if c == DC - 1 else None)
            xt_done[0] = s_pe.n
            se.wait(s_pe)
            for half in range(2):
                se.activation(
                    xq_fm[:, :].rearrange("p (c t) -> p c t", c=DC)[
                        :, half * 4:(half + 1) * 4, j * 128:(j + 1) * 128],
                    bk.ap[:, half * 512:(half + 1) * 512].rearrange(
                        "p (c t) -> p c t", c=4),
                    AF.Copy, inc=s_ae)
            bk.release(s_ae)
            te.wait(s_ae)
            nch = (II + 511) // 512
            for ch in range(nch):
                w0 = ch * 512
                w1 = min(II, w0 + 512)
                for (wT, bank) in ((wgT, pA), (wuT, pB)):
                    bank.acquire(te)
                    for ic in range(DC):
                        te.matmul(bank.ap[:, 0:w1 - w0],
                                  xq_fm[:, ic * QS + j * 128:
                                        ic * QS + (j + 1) * 128],
                                  wT[:, ic * II + w0:ic * II + w1],
                                  start=(ic == 0), stop=(ic == DC - 1),
                                  inc=s_pe if ic == DC - 1 else None)
                se.wait(s_pe)
                if ch == 0:
                    se.wait(s_vf)
                se.activation(g_sb[:, w0:w1], pA.ap[:, 0:w1 - w0], AF.Silu,
                              inc=s_ae)
                pA.release(s_ae)
                se.activation(h_sb[:, w0:w1], pB.ap[:, 0:w1 - w0], AF.Copy,
                              inc=s_ae)
                pB.release(s_ae)
            ve.wait(s_ae)
            P.vf.tensor_tensor(h_sb[:, :], g_sb[:, :], h_sb[:, :], ALU.mult)
            P.ve.wait(s_pe, xt_done[0])
            emit_quant(P, hq_sb[:, :], h_sb[:, :], scr, II, pool_cols=0)
            for blk in range(3):
                c0 = blk * 8
                c1 = min(NIC, c0 + 8)
                bk = pT[blk % 2]
                bk.acquire(te)
                if blk == 0:
                    te.wait(P.sem("s_vf"))
                    te.wait(P.sem("s_gp"))
                for c in range(c0, c1):
                    te.op("transpose", bk.ap[:, (c - c0) * 128:(c - c0 + 1) * 128],
                          hq_sb[:, c * 128:(c + 1) * 128], ident[:, :],
                          inc=s_pe if c == c1 - 1 else None)
                ht_done[0] = s_pe.n
                se.wait(s_pe)
                se.activation(hqT[:, c0 * 128:c1 * 128],
                              bk.ap[:, 0:(c1 - c0) * 128], AF.Copy, inc=s_ae)
                bk.release(s_ae)
            gj = g * 4 + j
            o_ps = o_ps2[gj % 2]
            o_ps.acquire(te)
            te.wait(s_ae)
            for half in range(2):
                for ic in range(NIC):
                    te.matmul(o_ps.ap[:, half * 512:(half + 1) * 512],
                              hqT[:, ic * 128:(ic + 1) * 128],
                              wdT[:, ic * 1024 + half * 512:
                                  ic * 1024 + (half + 1) * 512],
                              start=(ic == 0), stop=(ic == NIC - 1),
                              inc=s_pe if ic == NIC - 1 else None)
            down_done[gj] = s_pe.n
            if gj > 0:
                emit_deferred_ln(gj - 1)
    emit_deferred_ln(S // QS * 4 - 1)
    P._pool_finals = False
    if prep4 is not None:
        prep4.finish()
    sy.wait(d_out)
    P.barrier()
    phase_ctx.close()



class PrepJobs:

    def __init__(self, P, srcs, dr, pfx):
        self.P = P
        self.pfx = pfx
        self.jobs = []
        HID = 4096
        for nm, dst, O in (("h1", "h1wT", HID), ("h2", "h2wT", HID),
                           ("w3", "w3T", 1024)):
            for ic in range(HID // 128):
                self.jobs.append((srcs[nm], dr[dst], O, ic))
        self.idx = 0
        self.st_done = [0, 0]
        self.wt = None

    def alloc(self, ctx):
        nc = self.P.nc
        t = ctx.enter_context(
            nc.sbuf_tensor(self.pfx + "wprep0", [128, 2048], BF16))
        self.wt = [t, t]
        self.d_p = [self.P.sem(self.pfx + "d_p0")] * 2

    def emit(self, k):
        sy = self.P.sy
        end = min(len(self.jobs), self.idx + k)
        while self.idx < end:
            srcap, dst, O, ic = self.jobs[self.idx]
            b = self.idx % 2
            for o0 in range(0, O, 2048):
                o1 = min(O, o0 + 2048)
                sy.wait(self.d_p[b], self.st_done[b])
                sy.dma(self.wt[b][:, 0:o1 - o0],
                       srcap[o0:o1, ic * 128:(ic + 1) * 128],
                       inc=self.d_p[b], transpose=True)
                sy.wait(self.d_p[b], self.d_p[b].n)
                sy.dma(dst[ic, :, o0:o1], self.wt[b][:, 0:o1 - o0],
                       inc=self.d_p[b])
                self.st_done[b] = self.d_p[b].n
            self.idx += 1

    def finish(self):
        self.emit(len(self.jobs))
        self.P.sy.wait(self.d_p[0], self.st_done[0])
        self.P.sy.wait(self.d_p[1], self.st_done[1])


def emit_ffn1(P, cs, xsrc, ydst, srcs, dr, pfx):
    te, ve, se, ge, sy = P.te, P.ve, P.se, P.ge, P.sy
    nc = P.nc
    HID = 4096
    NHC = HID // 128
    phase_ctx = ExitStack()

    def sb(name, shape, dtype):
        return phase_ctx.enter_context(nc.sbuf_tensor(pfx + name, shape, dtype))

    def ps(name, shape, dtype=F32):
        return phase_ctx.enter_context(nc.psum_tensor(pfx + name, shape, dtype))

    d_w = P.sem(pfx + "d_w")
    d_x = [P.sem(pfx + f"d_x{i}") for i in range(2)]
    d_h = [P.sem(pfx + f"d_h{i}") for i in range(2)]
    d_s = P.sem(pfx + "d_s")
    d_out = P.sem("d_out")
    s_pe = P.sem("s_pe")
    s_ae = P.sem("s_ae")
    s_ve = P.sem("s_ve")

    sc_all = sb("scales", [128, 4 * 32 + 8], F32)
    s1v = sc_all[:, 0:32]
    s2v = sc_all[:, 32:64]
    sh1v = sc_all[:, 64:96]
    sh2v = sc_all[:, 96:128]
    for i, nm in enumerate(("s1", "s2", "sh1", "sh2")):
        sy.dma(sc_all[:, i * 32:(i + 1) * 32],
               srcs[nm].ap().rearrange("(c p) -> p c", p=128), inc=d_s,
               allow_slow_non_contiguous=True)
    s3row = sb("s3row", [1, 1024], F32)
    s3rowb = sb("s3rowb", [1, 1024], BF16)
    s3b = sb("s3b", [128, 1024], BF16)
    sy.dma(s3row[0:1, :], srcs["s3"].rearrange("(a d) -> a d", a=1), inc=d_s)
    se.wait(d_s)
    se.activation(s3rowb[0:1, :], s3row[0:1, :], AF.Copy, inc=s_ae)
    te.wait(s_ae)

    stA = ExitStack()

    def sbA(name, shape, dtype):
        return stA.enter_context(nc.sbuf_tensor(pfx + name, shape, dtype))

    w1T = sbA("w1T", [128, DC * HID], BF16)
    w2T = sbA("w2T", [128, DC * HID], BF16)
    x_fm = [sbA(f"xfm{i}", [128, DC * QS], BF16) for i in range(2)]
    g_sb = [sbA(f"gA{i}", [128, 512], BF16) for i in range(2)]
    u_sb = [sbA(f"uA{i}", [128, 512], BF16) for i in range(2)]
    h1blk = sbA("h1blk", [128, NHC * QS], BF16)
    pAb = [Bank(stA.enter_context(
        nc.psum_tensor(pfx + f"pA{i}", [128, 512], F32)).ap()) for i in range(2)]
    pBb = [Bank(stA.enter_context(
        nc.psum_tensor(pfx + f"pB{i}", [128, 512], F32)).ap()) for i in range(2)]
    for c in range(DC):
        sy.dma(w1T[:, c * HID:(c + 1) * HID],
               srcs["w1"][:, c * 128:(c + 1) * 128], inc=d_w, transpose=True)
        sy.dma(w2T[:, c * HID:(c + 1) * HID],
               srcs["w2"][:, c * 128:(c + 1) * 128], inc=d_w, transpose=True)
    te.wait(d_w)
    ve.wait(d_s)
    x_loads = [0, 0]
    for tb in range(S // QS):
        b = tb % 2
        if tb >= 2:
            sy.wait(s_pe)
        for c in range(DC):
            sy.dma(x_fm[b][:, c * QS:(c + 1) * QS],
                   xsrc[tb * QS:(tb + 1) * QS, c * 128:(c + 1) * 128],
                   inc=d_x[b], transpose=True)
        x_loads[b] = d_x[b].n
        te.wait(d_x[b], x_loads[b])
        mul_done = {}
        for hc in range(NHC):
            pA = pAb[hc % 2]
            pB = pBb[hc % 2]
            for (wT, bank) in ((w1T, pA), (w2T, pB)):
                bank.acquire(te)
                for ic in range(DC):
                    te.matmul(bank.ap,
                              wT[:, ic * HID + hc * 128:ic * HID + (hc + 1) * 128],
                              x_fm[b][:, ic * QS:(ic + 1) * QS],
                              start=(ic == 0), stop=(ic == DC - 1),
                              inc=s_pe if ic == DC - 1 else None)
            se.wait(s_pe)
            if hc >= 2:
                se.wait(P.sem("s_vf"), mul_done[hc - 2])
            se.activation(g_sb[hc % 2][:, :], pA.ap, AF.Silu,
                          scale=s1v[:, hc:hc + 1], inc=s_ae)
            pA.release(s_ae)
            se.activation(u_sb[hc % 2][:, :], pB.ap, AF.Copy,
                          scale=s2v[:, hc:hc + 1], inc=s_ae)
            pB.release(s_ae)
            ve.wait(s_ae)
            P.vf.tensor_tensor(h1blk[:, hc * QS:(hc + 1) * QS],
                               g_sb[hc % 2][:, :], u_sb[hc % 2][:, :], ALU.mult)
            mul_done[hc] = P.sem("s_vf").n
        sy.wait(P.sem("s_vf"))
        sy.dma(dr["h1s"][:, :, tb * QS:(tb + 1) * QS].rearrange(
            "c p t -> p c t"),
            h1blk[:, :].rearrange("p (c t) -> p c t", c=NHC), inc=d_h[0])
    sy.wait(d_h[0])
    P.barrier()
    stA.close()

    def big_stage(hin, hout, wtiled, scalev, TSUP, store):
        stB = ExitStack()

        def sbB(name, shape, dtype):
            return stB.enter_context(nc.sbuf_tensor(pfx + "B" + name, shape, dtype))

        nt = TSUP // 512
        hi = sbB("hi", [128, NHC * TSUP], BF16)
        ho = sbB("ho", [128, NHC * TSUP], BF16)
        wch = [sbB(f"wch{i}", [128, NHC * 256], BF16) for i in range(2)]
        pC = [Bank(stB.enter_context(
            nc.psum_tensor(pfx + f"pC{store}{i}", [128, 512], F32)).ap())
            for i in range(2)]
        d_hi = P.sem(pfx + "d_hi" + store)
        d_wc = [P.sem(pfx + f"d_wc{store}{i}") for i in range(2)]
        wc_loads = [0, 0]
        hi_loads = 0
        jo_pe_done_prev = [None]
        for tsup in range(S // TSUP):
            sy.wait(s_pe)
            sy.dma(hi[:, :].rearrange("p (c t) -> p c t", c=NHC),
                   hin[:, :, tsup * TSUP:(tsup + 1) * TSUP].rearrange(
                       "c p t -> p c t"), inc=d_hi)
            hi_loads = d_hi.n
            te.wait(d_hi, hi_loads)
            se.wait(d_hi, hi_loads)
            njo = HID // 256

            def ld_wch(jo, wait_pe):
                wb = jo % 2
                if wait_pe is not None:
                    sy.wait(s_pe, wait_pe)
                sy.dma(wch[wb][:, :].rearrange("p (c o) -> p c o", c=NHC),
                       wtiled[:, :, jo * 256:(jo + 1) * 256].rearrange(
                           "c p o -> p c o"), inc=d_wc[wb])
                wc_loads[wb] = d_wc[wb].n

            jo_pe_done = {}
            ld_wch(0, jo_pe_done_prev[0])
            ld_wch(1, jo_pe_done_prev[0])
            for jo in range(njo):
                wb = jo % 2
                te.wait(d_wc[wb], wc_loads[wb])
                for jt in range(2):
                    o0 = jo * 256 + jt * 128
                    for th in range(nt):
                        bank = pC[(jt * nt + th) % 2]
                        bank.acquire(te)
                        for ic in range(NHC):
                            te.matmul(
                                bank.ap,
                                wch[wb][:, ic * 256 + jt * 128:
                                        ic * 256 + (jt + 1) * 128],
                                hi[:, ic * TSUP + th * 512:
                                   ic * TSUP + (th + 1) * 512],
                                start=(ic == 0), stop=(ic == NHC - 1),
                                inc=s_pe if ic == NHC - 1 else None)
                        se.wait(s_pe)
                        se.activation(
                            ho[:, (o0 // 128) * TSUP + th * 512:
                               (o0 // 128) * TSUP + (th + 1) * 512],
                            bank.ap, AF.Gelu,
                            scale=scalev[:, o0 // 128:o0 // 128 + 1],
                            inc=s_ae)
                        bank.release(s_ae)
                jo_pe_done[jo] = s_pe.n
                if jo + 2 < njo:
                    ld_wch(jo + 2, jo_pe_done[jo])
            jo_pe_done_prev[0] = s_pe.n
            sy.wait(s_ae)
            sy.dma(hout[:, :, tsup * TSUP:(tsup + 1) * TSUP].rearrange(
                "c p t -> p c t"),
                ho[:, :].rearrange("p (c t) -> p c t", c=NHC), inc=d_hi)
        sy.wait(d_hi)
        P.barrier()
        stB.close()

    big_stage(dr["h1s"], dr["h2s"], dr["h1wT"], sh1v, 1024, "B")

    stC = ExitStack()

    def sbC(name, shape, dtype):
        return stC.enter_context(nc.sbuf_tensor(pfx + "C" + name, shape, dtype))

    TS2 = 512
    hi = sbC("hi", [128, NHC * TS2], BF16)
    h3 = sbC("h3", [128, NHC * TS2], BF16)
    wch = [sbC(f"wch{i}", [128, NHC * 256], BF16) for i in range(2)]
    w3T = sbC("w3T", [128, NHC * 1024], BF16)
    res_tm = [sbC(f"res{i}", [128, 1024], BF16) for i in range(2)]
    y_sb = sbC("y", [128, 1024], F32)
    yjunk = sbC("yj", [128, 1024], BF16)
    st8 = sbC("st8", [128, 8], F32)
    yout = [sbC(f"yo{i}", [128, 1024], F32) for i in range(2)]
    pC = [Bank(stC.enter_context(
        nc.psum_tensor(pfx + f"pD{i}", [128, 512], F32)).ap()) for i in range(2)]
    o_ps = Bank(stC.enter_context(
        nc.psum_tensor(pfx + "oD", [128, 1024], F32)).ap())
    psb = Bank(stC.enter_context(
        nc.psum_tensor(pfx + "psb", [128, 1024], F32)).ap())

    d_hi = P.sem(pfx + "d_hiC")
    d_wc = [P.sem(pfx + f"d_wcC{i}") for i in range(2)]
    d_w3 = P.sem(pfx + "d_w3")
    d_r = [P.sem(pfx + f"d_rC{i}") for i in range(2)]

    psb.acquire(te)
    for half in range(2):
        te.matmul(psb.ap[:, half * 512:(half + 1) * 512],
                  cs["ones1"][0:1, 0:128], s3rowb[0:1, half * 512:(half + 1) * 512],
                  start=True, stop=True, inc=s_pe)
    se.wait(s_pe)
    se.activation(s3b[:, :], psb.ap[:, :], AF.Copy, inc=s_ae)
    psb.release(s_ae)

    sy.dma(w3T[:, :].rearrange("p (c o) -> p c o", c=NHC),
           dr["w3T"][:, :, :].rearrange("c p o -> p c o"), inc=d_w3)
    te.wait(d_w3)
    wc_loads = [0, 0]
    r_loads = [0, 0]
    jo_pe_prev = [None]
    for tsup in range(S // TS2):
        sy.wait(s_pe)
        sy.dma(hi[:, :].rearrange("p (c t) -> p c t", c=NHC),
               dr["h2s"][:, :, tsup * TS2:(tsup + 1) * TS2].rearrange(
                   "c p t -> p c t"), inc=d_hi)
        te.wait(d_hi)
        njo = HID // 256

        def ld_wch2(jo, wait_pe):
            wb = jo % 2
            if wait_pe is not None:
                sy.wait(s_pe, wait_pe)
            sy.dma(wch[wb][:, :].rearrange("p (c o) -> p c o", c=NHC),
                   dr["h2wT"][:, :, jo * 256:(jo + 1) * 256].rearrange(
                       "c p o -> p c o"), inc=d_wc[wb])
            wc_loads[wb] = d_wc[wb].n

        jo_pe = {}
        ld_wch2(0, jo_pe_prev[0])
        ld_wch2(1, jo_pe_prev[0])
        for jo in range(njo):
            wb = jo % 2
            te.wait(d_wc[wb], wc_loads[wb])
            for jt in range(2):
                o0c = jo * 2 + jt
                bank = pC[o0c % 2]
                bank.acquire(te)
                for ic in range(NHC):
                    te.matmul(bank.ap,
                              wch[wb][:, ic * 256 + jt * 128:
                                      ic * 256 + (jt + 1) * 128],
                              hi[:, ic * TS2:(ic + 1) * TS2],
                              start=(ic == 0), stop=(ic == NHC - 1),
                              inc=s_pe if ic == NHC - 1 else None)
                se.wait(s_pe)
                se.activation(h3[:, o0c * TS2:(o0c + 1) * TS2], bank.ap,
                              AF.Gelu, scale=sh2v[:, o0c:o0c + 1], inc=s_ae)
                bank.release(s_ae)
            jo_pe[jo] = s_pe.n
            if jo + 2 < njo:
                ld_wch2(jo + 2, jo_pe[jo])
        jo_pe_prev[0] = s_pe.n
        te.wait(s_ae)
        for j in range(4):
            rb = (tsup * 4 + j) % 2
            sy.wait(s_ve)
            sy.dma(res_tm[rb][:, :],
                   xsrc[tsup * TS2 + j * 128:tsup * TS2 + (j + 1) * 128, :],
                   inc=d_r[rb])
            r_loads[rb] = d_r[rb].n
            o_ps.acquire(te)
            for half in range(2):
                for ic in range(NHC):
                    te.matmul(o_ps.ap[:, half * 512:(half + 1) * 512],
                              h3[:, ic * TS2 + j * 128:ic * TS2 + (j + 1) * 128],
                              w3T[:, ic * 1024 + half * 512:
                                  ic * 1024 + (half + 1) * 512],
                              start=(ic == 0), stop=(ic == NHC - 1),
                              inc=s_pe if ic == NHC - 1 else None)
            ve.wait(s_pe)
            ve.wait(d_r[rb], r_loads[rb])
            ve.wait(s_ae)
            ve.wait(d_out)
            emit_ln(P, yout[rb][:, :], o_ps.ap[:, :], res_tm[rb][:, :],
                    y_sb, yjunk, st8, inc=s_ve, premul=s3b[:, :])
            o_ps.release(s_ve)
            sy.wait(s_ve)
            sy.dma(ydst[tsup * TS2 + j * 128:tsup * TS2 + (j + 1) * 128, :],
                   yout[rb][:, :], inc=d_out)
    sy.wait(d_out)
    P.barrier()
    stC.close()
    phase_ctx.close()


def emit_cast(P, dst, src, rows, chunk=4096):
    d_cast = P.sem("d_cast")
    for r0 in range(0, rows, chunk):
        r1 = min(rows, r0 + chunk)
        P.ge.dma(dst[r0:r1, :], src[r0:r1, :], inc=d_cast)
    return d_cast


def build_launch_a():
    P = Prog("mkA")
    x = P.dram("x", [S, D], F32, kind="ExternalInput")
    wqkv = P.dram("wqkv", [3072, D], F32, kind="ExternalInput")
    wo = P.dram("wo", [D, D], F32, kind="ExternalInput")
    xs1 = P.dram("xs1", [S, D], BF16, kind="ExternalOutput")
    xs0 = P.dram("xs0", [S, D], BF16)
    bqkv = P.dram("bqkv", [3072, D], BF16)
    bwo = P.dram("bwo", [D, D], BF16)
    cs = emit_consts(P)
    d_cast = emit_cast(P, xs0, x, S)
    emit_cast(P, bqkv, wqkv, 3072)
    emit_cast(P, bwo, wo, D)
    P.sy.wait(d_cast)
    emit_attn(P, cs, xs0, xs1, {"qkv": bqkv, "o": bwo}, "l0", "p1_")
    return P.realize()


def build_launch_b():
    P = Prog("mkB")
    xs1 = P.dram("xs1", [S, D], BF16, kind="ExternalInput")
    wg = P.dram("wg", [2816, D], F32, kind="ExternalInput")
    wu = P.dram("wu", [2816, D], F32, kind="ExternalInput")
    wd = P.dram("wd", [D, 2816], F32, kind="ExternalInput")
    xs2 = P.dram("xs2", [S, D], BF16, kind="ExternalOutput")
    bg = P.dram("bg", [2816, D], BF16)
    bu = P.dram("bu", [2816, D], BF16)
    bd = P.dram("bd", [D, 2816], BF16)
    xq_s = P.dram("xq_s", [S, D], BF16)
    hq_s = P.dram("hq_s", [S, 2816], BF16)
    cs = emit_consts(P)
    d_prep = P.sem("d_prep")
    qctx = ExitStack()
    nc = P.nc
    scr = {}
    for nm, shape, dt_ in (("W", [128, 2816], F32), ("W2", [128, 2816], F32),
                           ("WQ", [128, 2816], BF16),
                           ("A", [128, 2816], BF16), ("B", [128, 2816], BF16),
                           ("C", [128, 2816], BF16), ("D", [128, 2816], BF16),
                           ("E", [128, 2816], BF16), ("mx", [128, 176], F32),
                           ("sc", [128, 176], F32), ("rs", [128, 176], F32)):
        scr[nm] = qctx.enter_context(nc.sbuf_tensor("qp_" + nm, shape, dt_))
    emit_qprep(P, wg, bg, 2816, 1024, scr, d_prep)
    emit_qprep(P, wu, bu, 2816, 1024, scr, d_prep)
    emit_qprep(P, wd, bd, 1024, 2816, scr, d_prep)
    P.sy.wait(d_prep)
    P.barrier()
    qctx.close()
    emit_ffn0(P, cs, xs1, xs2, {"gate": bg, "up": bu, "down": bd}, {}, "p2_")
    return P.realize()


def build_launch_c():
    P = Prog("mkC")
    xs2 = P.dram("xs2", [S, D], BF16, kind="ExternalInput")
    ws = {nm: P.dram("w" + nm, [D, D], F32, kind="ExternalInput")
          for nm in ("q", "k", "v", "o")}
    xs3 = P.dram("xs3", [S, D], BF16, kind="ExternalOutput")
    bs = {nm: P.dram("b" + nm, [D, D], BF16) for nm in ("q", "k", "v", "o")}
    cs = emit_consts(P)
    d_cast = None
    for nm in ("q", "k", "v", "o"):
        d_cast = emit_cast(P, bs[nm], ws[nm], D)
    P.sy.wait(d_cast)
    emit_attn(P, cs, xs2, xs3, bs, "l1", "p3_")
    return P.realize()


def build_launch_d():
    P = Prog("mkD")
    HID = 4096
    xs3 = P.dram("xs3", [S, D], BF16, kind="ExternalInput")
    y = P.dram("y", [S, D], F32, kind="ExternalOutput")
    srcs = {}
    fsrc = {}
    for nm, shape in (("w1", [HID, D]), ("w2", [HID, D]), ("h1", [HID, HID]),
                      ("h2", [HID, HID]), ("w3", [D, HID])):
        fsrc[nm] = P.dram("f_" + nm, shape, F32, kind="ExternalInput")
        srcs[nm] = P.dram("b_" + nm, shape, BF16)
    for nm, n in (("s1", HID), ("s2", HID), ("sh1", HID), ("sh2", HID),
                  ("s3", D)):
        srcs[nm] = P.dram("v_" + nm, [n], F32, kind="ExternalInput")
    dr = {
        "h1s": P.dram("h1s", [32, 128, S], BF16),
        "h2s": P.dram("h2s", [32, 128, S], BF16),
        "h1wT": P.dram("h1wT", [32, 128, HID], BF16),
        "h2wT": P.dram("h2wT", [32, 128, HID], BF16),
        "w3T": P.dram("w3T", [32, 128, D], BF16),
    }
    cs = emit_consts(P)
    d_cast = None
    for nm in ("w1", "w2", "h1", "h2", "w3"):
        d_cast = emit_cast(P, srcs[nm], fsrc[nm], fsrc[nm].shape[0])
    P.sy.wait(d_cast)
    emit_ffn1(P, cs, xs3, y, srcs, dr, "p4_")
    return P.realize()


def kernel_unfused(L0_attn_x, L0_attn_W_qkv, L0_attn_W_o, L0_ffn_w_gate, L0_ffn_w_up,
           L0_ffn_w_down, L1_attn_Wq, L1_attn_Wk, L1_attn_Wv, L1_attn_Wo,
           L1_ffn_w1_w, L1_ffn_w1_s, L1_ffn_w2_w, L1_ffn_w2_s, L1_ffn_h1_w,
           L1_ffn_h1_s, L1_ffn_h2_w, L1_ffn_h2_s, L1_ffn_w3_w, L1_ffn_w3_s):
    n = 8
    consts = make_consts()
    x = np.ascontiguousarray(np.asarray(L0_attn_x, dtype=np.float32))
    cores = list(range(n))

    def rep(arr):
        a = np.ascontiguousarray(np.asarray(arr, dtype=np.float32))
        return [a] * n

    nc_a = build_launch_a()
    in_a = [dict(x=x[b], wqkv=rep(L0_attn_W_qkv)[b], wo=rep(L0_attn_W_o)[b],
                 **consts) for b in range(n)]
    res_a = run_bass_kernel_spmd(nc_a, in_a, core_ids=cores)
    xs1 = [res_a.results[b]["xs1"] for b in range(n)]

    nc_b = build_launch_b()
    in_b = [dict(xs1=xs1[b], wg=rep(L0_ffn_w_gate)[b], wu=rep(L0_ffn_w_up)[b],
                 wd=rep(L0_ffn_w_down)[b], **consts) for b in range(n)]
    res_b = run_bass_kernel_spmd(nc_b, in_b, core_ids=cores)
    xs2 = [res_b.results[b]["xs2"] for b in range(n)]

    nc_c = build_launch_c()
    in_c = [dict(xs2=xs2[b], wq=rep(L1_attn_Wq)[b], wk=rep(L1_attn_Wk)[b],
                 wv=rep(L1_attn_Wv)[b], wo=rep(L1_attn_Wo)[b], **consts)
            for b in range(n)]
    res_c = run_bass_kernel_spmd(nc_c, in_c, core_ids=cores)
    xs3 = [res_c.results[b]["xs3"] for b in range(n)]

    nc_d = build_launch_d()
    in_d = [dict(xs3=xs3[b], f_w1=rep(L1_ffn_w1_w)[b], f_w2=rep(L1_ffn_w2_w)[b],
                 f_h1=rep(L1_ffn_h1_w)[b], f_h2=rep(L1_ffn_h2_w)[b],
                 f_w3=rep(L1_ffn_w3_w)[b], v_s1=rep(L1_ffn_w1_s)[b],
                 v_s2=rep(L1_ffn_w2_s)[b], v_sh1=rep(L1_ffn_h1_s)[b],
                 v_sh2=rep(L1_ffn_h2_s)[b], v_s3=rep(L1_ffn_w3_s)[b],
                 **consts) for b in range(n)]
    res_d = run_bass_kernel_spmd(nc_d, in_d, core_ids=cores)
    y = np.stack([np.asarray(res_d.results[b]["y"]) for b in range(n)], axis=0)
    return y


def build_fused():
    P = Prog("mk")
    HID = 4096
    x = P.dram("x", [S, D], F32, kind="ExternalInput")
    y = P.dram("y", [S, D], F32, kind="ExternalOutput")
    wqkv = P.dram("wqkv", [3072, D], F32, kind="ExternalInput")
    wo0 = P.dram("wo0", [D, D], F32, kind="ExternalInput")
    wg = P.dram("wg", [2816, D], F32, kind="ExternalInput")
    wu = P.dram("wu", [2816, D], F32, kind="ExternalInput")
    wd = P.dram("wd", [D, 2816], F32, kind="ExternalInput")
    w1a = {nm: P.dram("w" + nm, [D, D], F32, kind="ExternalInput")
           for nm in ("q", "k", "v", "o")}
    fsrc = {}
    srcs4 = {}
    for nm, shape in (("w1", [HID, D]), ("w2", [HID, D]), ("h1", [HID, HID]),
                      ("h2", [HID, HID]), ("w3", [D, HID])):
        fsrc[nm] = P.dram("f_" + nm, shape, F32, kind="ExternalInput")
        srcs4[nm] = P.dram("b_" + nm, shape, BF16)
    for nm, nn in (("s1", HID), ("s2", HID), ("sh1", HID), ("sh2", HID),
                   ("s3", D)):
        srcs4[nm] = P.dram("v_" + nm, [nn], F32, kind="ExternalInput")
    xs0 = P.dram("xs0", [S, D], BF16)
    xs1 = P.dram("xs1", [S, D], BF16)
    xs2 = P.dram("xs2", [S, D], BF16)
    xs3 = P.dram("xs3", [S, D], BF16)
    bqkv = P.dram("bqkv", [3072, D], BF16)
    bwo0 = P.dram("bwo0", [D, D], BF16)
    bg = P.dram("bg", [2816, D], BF16)
    bu = P.dram("bu", [2816, D], BF16)
    bd = P.dram("bd", [D, 2816], BF16)
    b1a = {nm: P.dram("b1" + nm, [D, D], BF16) for nm in ("q", "k", "v", "o")}
    dr4 = {
        "h1s": P.dram("h1s", [32, 128, S], BF16),
        "h2s": P.dram("h2s", [32, 128, S], BF16),
        "h1wT": P.dram("h1wT", [32, 128, HID], BF16),
        "h2wT": P.dram("h2wT", [32, 128, HID], BF16),
        "w3T": P.dram("w3T", [32, 128, D], BF16),
    }

    cs = emit_consts(P)
    d_cast = emit_cast(P, bqkv, wqkv, 3072)
    emit_cast(P, bwo0, wo0, D)
    c1 = d_cast.n
    xc = []
    d_xc = P.sem("d_xc")
    for s0 in range(NQS):
        P.ge.dma(xs0[s0 * QS:(s0 + 1) * QS, :], x[s0 * QS:(s0 + 1) * QS, :],
                 inc=d_xc)
        xc.append(d_xc.n)
    for nm in ("q", "k", "v", "o"):
        emit_cast(P, b1a[nm], w1a[nm], D)
    c3 = d_cast.n
    for nm in ("w1", "w2", "h1", "h2", "w3"):
        emit_cast(P, srcs4[nm], fsrc[nm], fsrc[nm].shape[0])
    c4 = d_cast.n

    d_prep = P.sem("d_prep")
    qp = QPrep(P, [(wg, bg, 2816, 1024), (wu, bu, 2816, 1024),
                   (wd, bd, 1024, 2816)], d_prep)
    P._p1_qprep = qp
    P.sy.wait(d_cast, c1)
    emit_attn(P, cs, xs0, xs1, {"qkv": bqkv, "o": bwo0}, "l0", "p1_",
              xgate=lambda s: P.sy.wait(d_xc, xc[s]))
    qp.finish()

    prep4 = PrepJobs(P, srcs4, dr4, "p4p_")
    P._prep4 = prep4
    P._prep4_c4 = c4
    emit_ffn0(P, cs, xs1, xs2, {"gate": bg, "up": bu, "down": bd}, {}, "p2_")

    P.sy.wait(d_cast, c3)
    emit_attn(P, cs, xs2, xs3, b1a, "l1", "p3_")

    prep4.finish()
    P.barrier()
    emit_ffn1(P, cs, xs3, y, srcs4, dr4, "p4_")
    return P.realize()


def kernel_fused(L0_attn_x, L0_attn_W_qkv, L0_attn_W_o, L0_ffn_w_gate,
                 L0_ffn_w_up, L0_ffn_w_down, L1_attn_Wq, L1_attn_Wk,
                 L1_attn_Wv, L1_attn_Wo, L1_ffn_w1_w, L1_ffn_w1_s,
                 L1_ffn_w2_w, L1_ffn_w2_s, L1_ffn_h1_w, L1_ffn_h1_s,
                 L1_ffn_h2_w, L1_ffn_h2_s, L1_ffn_w3_w, L1_ffn_w3_s):
    n = 8
    consts = make_consts()
    x = np.ascontiguousarray(np.asarray(L0_attn_x, dtype=np.float32))

    def f32(a):
        return np.ascontiguousarray(np.asarray(a, dtype=np.float32))

    nc = build_fused()
    shared = dict(wqkv=f32(L0_attn_W_qkv), wo0=f32(L0_attn_W_o),
                  wg=f32(L0_ffn_w_gate), wu=f32(L0_ffn_w_up),
                  wd=f32(L0_ffn_w_down), wq=f32(L1_attn_Wq),
                  wk=f32(L1_attn_Wk), wv=f32(L1_attn_Wv), wo=f32(L1_attn_Wo),
                  f_w1=f32(L1_ffn_w1_w), f_w2=f32(L1_ffn_w2_w),
                  f_h1=f32(L1_ffn_h1_w), f_h2=f32(L1_ffn_h2_w),
                  f_w3=f32(L1_ffn_w3_w), v_s1=f32(L1_ffn_w1_s),
                  v_s2=f32(L1_ffn_w2_s), v_sh1=f32(L1_ffn_h1_s),
                  v_sh2=f32(L1_ffn_h2_s), v_s3=f32(L1_ffn_w3_s), **consts)
    in_maps = [dict(x=x[b], **shared) for b in range(n)]
    res = run_bass_kernel_spmd(nc, in_maps, core_ids=list(range(n)))
    return np.stack([np.asarray(res.results[b]["y"]) for b in range(n)], axis=0)


kernel = kernel_fused
